# version 01 (final)
# baseline (speedup 1.0000x reference)
import math
import numpy as np
import ml_dtypes
import concourse.bass as bass
import concourse.mybir as mybir
from concourse.bass_utils import run_bass_kernel_spmd

F32 = mybir.dt.float32
BF16 = mybir.dt.bfloat16
AF = mybir.ActivationFunctionType
ALU = mybir.AluOpType
bfloat16 = ml_dtypes.bfloat16

BATCH, SEQ, D = 4, 4096, 2048
DI, NST, DTR = 4096, 16, 128
NH, NKV, HD = 16, 4, 128
WINDOW = 2048
INTER = 8192
NEXP, MOE_I, SH_I = 16, 1024, 4096
LAMBDA_INIT = 0.8 - 0.6 * float(np.exp(-0.3))
EPS = 1e-5
CH = 512
NCH = SEQ // CH
POST0 = 2048
PCH0 = POST0 // CH
KW = WINDOW + 128
NEG = -1e30
SCL = 1.0 / math.sqrt(HD)
A_MG = 8
A_XP = True
A_DT = True
A_CONV = True
S_DBG = None
T_DBG = None
T_DBG2 = None
T_DBG3 = None
T_DBG4 = None


def _t(buf, t, w):
    return buf[:, t * w:(t + 1) * w]


class Tok:
    __slots__ = ("ctr", "val")
    def __init__(self, ctr, val):
        self.ctr = ctr
        self.val = val


class Ctr:
    def __init__(self, handle, name):
        self.h = handle
        self.name = name
        self.n = 0

    def mark(self, instr, k=1):
        instr.then_inc(self.h, k)
        self.n += k
        return Tok(self, self.n)

    def tok(self):
        return Tok(self, self.n)


def wait(eng, *toks):
    for tk in toks:
        if tk is not None and tk.val > 0:
            eng.wait_ge(tk.ctr.h, tk.val)


def em(adder, fn):
    box = []
    adder(lambda e: box.append(fn(e)))
    return box[0]


class Bld:
    def __init__(self, nc, st):
        self.nc = nc
        self.st = st
        self.fence = []
        self._ns = 0

    def sem(self, name=None):
        if name is None:
            name = f"s{self._ns}"
            self._ns += 1
        return Ctr(self.st.enter_context(self.nc.semaphore(name)), name)

    def sb(self, name, shape, dtype):
        return self.st.enter_context(self.nc.sbuf_tensor(name, list(shape), dtype))


def build_program(dev_taps=(), upto=None, races=True):
    nc = bass.Bass("TRN2", target_bir_lowering=False, debug=False,
                   detect_race_conditions=races)
    import contextlib
    stack = contextlib.ExitStack()

    def kind_of(name):
        return "ExternalOutput" if name in dev_taps else "Internal"

    def din(name, shape, dtype=F32):
        return nc.dram_tensor(name, list(shape), dtype, kind="ExternalInput").ap()

    def dsc(name, shape, dtype=F32):
        return nc.dram_tensor(name, list(shape), dtype, kind=kind_of(name)).ap()

    g = {}
    g["x_in"] = din("x_sl", [D, SEQ])
    g["valid128"] = din("valid128", [128, SEQ])
    g["npadneg"] = din("npadneg", [128, 16])
    g["attn_bnd"] = din("attn_bnd", [128, 256], BF16)
    g["ident_in"] = din("ident", [128, 128], BF16)
    g["ident_f_in"] = din("ident_f", [128, 128], F32)
    g["inproj_T"] = din("inproj_T", [D, 2 * DI])
    g["conv_w"] = din("conv_w", [128, 128])
    g["conv_b"] = din("conv_b", [128, 32])
    g["xproj_Tp"] = din("xproj_Tp", [DI, 256])
    g["dtproj_T"] = din("dtproj_T", [DTR, DI])
    g["dtproj_b"] = din("dtproj_b", [128, 32])
    g["A_log"] = din("A_log_t", [128, 32 * NST])
    g["D_par"] = din("D_par", [128, 32])
    g["mambaout_T"] = din("mambaout_T", [DI, D])
    g["mamba_nw"] = din("mamba_nw", [128, 16])
    g["attn_nw"] = din("attn_nw", [128, 16])
    g["mlp_nw"] = din("mlp_nw", [128, 16])
    g["wqkv_T"] = din("wqkv_T", [D, 3072])
    g["wqkv_b"] = din("wqkv_b", [128, 24])
    g["outproj_T"] = din("outproj_T", [D, D])
    g["outproj_b"] = din("outproj_b", [128, 16])
    g["subln_pat"] = din("subln_pat", [128, 2])
    g["lam_q1"] = din("lam_q1", [128, 1])
    g["lam_k1"] = din("lam_k1", [128, 1])
    g["lam_q2"] = din("lam_q2", [128, 1])
    g["lam_k2"] = din("lam_k2", [128, 1])
    g["fc1_T"] = din("fc1_T", [D, 2 * INTER])
    g["fc2_T"] = din("fc2_T", [INTER, D])
    g["router_T"] = din("router_T", [D, NEXP])
    g["ebias_rep"] = din("ebias_rep", [128, NEXP])
    g["wup_cat"] = din("wup_cat", [D, NEXP * MOE_I])
    g["wdown_cat"] = din("wdown_cat", [NEXP * MOE_I, D])
    g["sharedup"] = din("sharedup", [D, SH_I])
    g["shareddown"] = din("shareddown", [SH_I, D])

    g["sc_xn"] = dsc("sc_xn", [D, SEQ])
    g["sc_xnw"] = dsc("sc_xnw", [D, SEQ], BF16)
    g["sc_xs"] = dsc("sc_xs", [DI, SEQ])
    g["sc_sz"] = dsc("sc_sz", [DI, SEQ], BF16)
    g["sc_dt"] = dsc("sc_dt", [DI, SEQ])
    g["sc_bc"] = dsc("sc_bc", [32, SEQ])
    g["sc_yz"] = dsc("sc_yz", [DI, SEQ], BF16)
    g["sc_h"] = dsc("sc_h", [D, POST0])
    g["sc_q"] = dsc("sc_q", [D, POST0], BF16)
    g["sc_h2"] = dsc("sc_h2", [D, POST0])
    g["sc_h2nw"] = dsc("sc_h2nw", [D, POST0], BF16)
    g["sc_glu"] = dsc("sc_glu", [INTER, POST0], BF16)
    g["sc_r1"] = dsc("sc_r1", [D, POST0])
    g["sc_x2w"] = dsc("sc_x2w", [D, POST0], BF16)
    g["sc_a"] = dsc("sc_a", [NEXP * MOE_I, POST0], BF16)
    g["sc_acc"] = dsc("sc_acc", [D, POST0])
    g["sc_kc"] = dsc("sc_kc", [NKV * HD, SEQ], BF16)
    g["sc_vtc"] = dsc("sc_vtc", [128, 2 * NKV * 16 * HD], BF16)
    g["sc_comb"] = dsc("sc_comb", [NEXP, POST0])
    g["sc_dbg"] = dsc("sc_dbg", [128, 3 * NST * CH])
    g["out_t"] = nc.dram_tensor("out_t", [D, POST0], F32, kind="ExternalOutput").ap()

    with stack:
        b = Bld(nc, stack)
        P = {}
        P["ones_bf"] = b.sb("p_ones_bf", [128, 1], BF16)
        P["ones_fc"] = b.sb("p_ones_fc", [128, 1], F32)
        P["eps"] = b.sb("p_eps", [128, 1], F32)
        P["ones_f"] = b.sb("p_ones_f", [1, 128], F32)
        P["ident"] = b.sb("p_ident", [128, 128], BF16)
        P["ident_f"] = b.sb("p_ident_f", [128, 128], F32)
        P["bnd"] = b.sb("p_bnd", [128, 256], BF16)
        P["convw"] = b.sb("p_convw", [128, 32 * 4], F32)
        P["convb"] = b.sb("p_convb", [128, 32], F32)
        P["dtb"] = b.sb("p_dtb", [128, 32], F32)
        P["Aneg"] = b.sb("p_Aneg", [128, 32 * NST], F32)
        P["Dpar"] = b.sb("p_Dpar", [128, 32], F32)
        P["mnw"] = b.sb("p_mnw", [128, 16], F32)
        P["anw"] = b.sb("p_anw", [128, 16], F32)
        P["pnw"] = b.sb("p_pnw", [128, 16], F32)
        P["qkvb"] = b.sb("p_qkvb", [128, 24], F32)
        P["opb"] = b.sb("p_opb", [128, 16], F32)
        P["subln"] = b.sb("p_subln", [128, 2], F32)
        P["lam"] = b.sb("p_lam", [128, 1], F32)
        P["lamn"] = b.sb("p_lamn", [128, 1], F32)
        P["npad"] = b.sb("p_npad", [128, 16], F32)
        P["carry"] = b.sb("p_carry", [128, 32 * NST], F32)
        P["ebias"] = b.sb("p_ebias", [128, NEXP], F32)
        P["valid"] = b.sb("p_valid", [128, SEQ], F32)

        _phase_p0(nc, b, g, P)
        phases = ["P0"]
        for name, fn in [("A0", _phase_a0), ("A", _phase_a), ("S", _phase_scan),
                         ("M", _phase_m), ("ATT", _phase_att), ("MLP", _phase_mlp),
                         ("MOE", _phase_moe)]:
            if upto is not None and upto in phases:
                break
            fn(nc, b, g, P)
            phases.append(name)

        with nc.Block() as blk:
            @blk.sync
            def _(s):
                wait(s, *b.fence)
    return nc


def _phase_p0(nc, b, g, P):
    ld = b.sem("p0ld")
    vs = b.sem("p0v")
    ss = b.sem("p0s")
    ts = b.sem("p0t")
    t_lq = b.sb("p0_lq", [128, 4], F32)
    t_lam = b.sb("p0_lam", [1, 4], F32)
    import contextlib as _cl
    _p0sk = _cl.ExitStack()
    ps_p0 = _p0sk.enter_context(nc.psum_tensor("p0_ps", [128, 8], F32))


    with nc.Block() as blk:
        def SY(f): blk.sync(f)
        def V(f): blk.vector(f)
        def SC(f): blk.scalar(f)
        def TE(f): blk.tensor(f)

        def _loads(s):
            for dst, src in [
                (P["ident"][:], g["ident_in"][:, :]),
                (P["ident_f"][:], g["ident_f_in"][:, :]),
                (P["bnd"][:], g["attn_bnd"][:, :]),
                (P["convw"][:], g["conv_w"][:, :]), (P["convb"][:], g["conv_b"][:, :]),
                (P["dtb"][:], g["dtproj_b"][:, :]), (P["Aneg"][:], g["A_log"][:, :]),
                (P["Dpar"][:], g["D_par"][:, :]), (P["mnw"][:], g["mamba_nw"][:, :]),
                (P["anw"][:], g["attn_nw"][:, :]), (P["pnw"][:], g["mlp_nw"][:, :]),
                (P["qkvb"][:], g["wqkv_b"][:, :]), (P["opb"][:], g["outproj_b"][:, :]),
                (P["subln"][:], g["subln_pat"][:, :]), (P["npad"][:], g["npadneg"][:, :]),
                (P["ebias"][:], g["ebias_rep"][:, :]), (P["valid"][:], g["valid128"][:, :]),
                (t_lq[:, 0:1], g["lam_q1"][:, :]), (t_lq[:, 1:2], g["lam_k1"][:, :]),
                (t_lq[:, 2:3], g["lam_q2"][:, :]), (t_lq[:, 3:4], g["lam_k2"][:, :]),
            ]:
                ld.mark(s.dma_start(out=dst, in_=src), 16)
        SY(_loads)

        def _v0(v):
            v.memset(P["ones_bf"][:], 1.0)
            v.memset(P["ones_fc"][:], 1.0)
            v.memset(P["eps"][:], EPS)
            vs.mark(v.memset(P["ones_f"][:], 1.0))
            vs.mark(v.memset(P["carry"][:], 0.0))
        V(_v0)

        def _t0(t):
            wait(t, ld.tok())
            t.matmul(ps_p0[0:1, 0:1], t_lq[:, 0:1], t_lq[:, 1:2], start=True, stop=True)
            ts.mark(t.matmul(ps_p0[0:1, 1:2], t_lq[:, 2:3], t_lq[:, 3:4],
                             start=True, stop=True))
        TE(_t0)

        def _s0(s):
            wait(s, ts.tok(), ld.tok())
            s.activation(t_lam[0:1, 0:2], ps_p0[0:1, 0:2], AF.Exp,
                         bias=0.0, scale=1.0)
            ss.mark(s.activation(P["Aneg"][:, :], P["Aneg"][:, :], AF.Exp,
                                 bias=0.0, scale=1.0))
        SC(_s0)

        def _v1(v):
            wait(v, ss.tok())
            v.tensor_tensor(t_lam[0:1, 2:3], t_lam[0:1, 0:1], t_lam[0:1, 1:2],
                            ALU.subtract)
            vs.mark(v.tensor_scalar(P["Aneg"][:, :], P["Aneg"][:, :], -1.0, None,
                                    ALU.mult))
            vs.mark(v.tensor_scalar(P["subln"][:, :], P["subln"][:, :],
                                    1.0 - LAMBDA_INIT, None, ALU.mult))
            vs.mark(v.tensor_scalar(t_lam[0:1, 2:3], t_lam[0:1, 2:3], LAMBDA_INIT,
                                    None, ALU.add))
        V(_v1)

        def _t1(t):
            wait(t, vs.tok())
            ts.mark(t.matmul(ps_p0[0:128, 2:3], P["ones_f"][0:1, :], t_lam[0:1, 2:3],
                             start=True, stop=True))
        TE(_t1)

        def _v2(v):
            wait(v, ts.tok())
            vs.mark(v.tensor_copy(P["lam"][:, :], ps_p0[:, 2:3]))
            vs.mark(v.tensor_scalar(P["lamn"][:, :], ps_p0[:, 2:3], -1.0, None,
                                    ALU.mult))
        V(_v2)

    _p0sk.close()
    b.fence = [vs.tok(), ss.tok(), ts.tok(), ld.tok()]


def _phase_a0(nc, b, g, P):
    fence = b.fence
    ld = b.sem("a0ld")
    vs = b.sem("a0v")
    ss = b.sem("a0s")
    ts = b.sem("a0t")
    st1 = b.sem("a0st1")
    st2 = b.sem("a0st2")
    import contextlib
    with contextlib.ExitStack() as sk:
        SB = lambda n, sh, dt: sk.enter_context(nc.sbuf_tensor(n, list(sh), dt))
        xb = [SB(f"a0_x{i}", [128, 16 * CH], F32) for i in range(2)]
        sqb = SB("a0_sq", [128, 16 * CH], BF16)
        xn = [SB("a0_xn0", [128, 16 * CH], F32)] * 2
        xnw = [SB("a0_xnw0", [128, 16 * CH], BF16)] * 2
        stat = SB("a0_stat", [1, 4 * CH], F32)
        ps_m = sk.enter_context(nc.psum_tensor("a0_psm", [128, CH], F32))
        ps_s = sk.enter_context(nc.psum_tensor("a0_pss", [128, CH], F32))
        ps_b1 = sk.enter_context(nc.psum_tensor("a0_psb1", [128, CH], F32))
        ps_b2 = sk.enter_context(nc.psum_tensor("a0_psb2", [128, CH], F32))
        MEAN, VAR, RST1, RST2 = 0, 1, 2, 3

        t_ld = [None] * NCH
        t_sq1 = [None] * NCH
        t_stats = [None] * NCH
        t_rst1 = [None] * NCH
        t_xn = [None] * NCH
        t_sq2 = [None] * NCH
        t_rst2 = [None] * NCH
        t_msum = [None] * NCH
        t_ssum = [None] * NCH
        t_bc1 = [None] * NCH
        t_ssum2 = [None] * NCH
        t_bc2 = [None] * NCH
        t_xnw = [None] * NCH
        t_st1 = [None] * NCH
        t_st2 = [None] * NCH

        with nc.Block() as blk:
            for c in range(NCH):
                buf = c % 2

                def _ld(s, c=c, buf=buf):
                    if c < 2:
                        wait(s, *fence)
                    else:
                        wait(s, t_xn[c - 2])
                    t_ld[c] = ld.mark(s.dma_start(
                        out=xb[buf][:].rearrange("p (t c) -> p t c", c=CH),
                        in_=g["x_in"][:, c * CH:(c + 1) * CH]
                            .rearrange("(t p) c -> p t c", p=128)), 16)
                blk.sync(_ld)

                def _ssq(s, c=c, buf=buf):
                    if c == 0:
                        wait(s, *fence)
                    wait(s, t_ld[c])
                    if c >= 1:
                        wait(s, t_ssum[c - 1])
                    for t in range(16):
                        i = s.activation(_t(sqb, t, CH), _t(xb[buf], t, CH), AF.Square,
                                         bias=0.0, scale=1.0)
                    t_sq1[c] = ss.mark(i)
                blk.scalar(_ssq)

                def _tsum(t, c=c, buf=buf):
                    if c == 0:
                        wait(t, *fence)
                    wait(t, t_ld[c])
                    if c >= 1:
                        wait(t, t_stats[c - 1])
                    for k in range(16):
                        i = t.matmul(ps_m[0:1, :], P["ones_fc"][:, :], _t(xb[buf], k, CH),
                                     start=(k == 0), stop=(k == 15))
                    t_msum[c] = ts.mark(i)
                    wait(t, t_sq1[c])
                    if c >= 1:
                        wait(t, t_rst2[c - 1])
                    for k in range(16):
                        i = t.matmul(ps_s[0:1, :], P["ones_bf"][:, :], _t(sqb, k, CH),
                                     start=(k == 0), stop=(k == 15))
                    t_ssum[c] = ts.mark(i)
                blk.tensor(_tsum)

                def _vstat(v, c=c):
                    wait(v, t_msum[c], t_ssum[c])
                    v.tensor_scalar(stat[0:1, MEAN * CH:(MEAN + 1) * CH], ps_m[0:1, :],
                                    1.0 / D, None, ALU.mult)
                    v.tensor_scalar(stat[0:1, VAR * CH:(VAR + 1) * CH], ps_s[0:1, :],
                                    1.0 / D, None, ALU.mult)
                    v.tensor_tensor(stat[0:1, RST1 * CH:(RST1 + 1) * CH],
                                    stat[0:1, MEAN * CH:(MEAN + 1) * CH],
                                    stat[0:1, MEAN * CH:(MEAN + 1) * CH], ALU.mult)
                    i = v.tensor_tensor(stat[0:1, VAR * CH:(VAR + 1) * CH],
                                        stat[0:1, VAR * CH:(VAR + 1) * CH],
                                        stat[0:1, RST1 * CH:(RST1 + 1) * CH],
                                        ALU.subtract)
                    t_stats[c] = vs.mark(i)
                blk.vector(_vstat)

                def _srst1(s, c=c):
                    wait(s, t_stats[c])
                    s.activation(stat[0:1, RST1 * CH:(RST1 + 1) * CH],
                                 stat[0:1, VAR * CH:(VAR + 1) * CH], AF.Ln,
                                 bias=P["eps"][0:1, :], scale=1.0)
                    i = s.activation(stat[0:1, RST1 * CH:(RST1 + 1) * CH],
                                     stat[0:1, RST1 * CH:(RST1 + 1) * CH], AF.Exp,
                                     bias=0.0, scale=-0.5)
                    t_rst1[c] = ss.mark(i)
                blk.scalar(_srst1)

                def _tbc1(t, c=c):
                    wait(t, t_rst1[c])
                    if c >= 1:
                        wait(t, t_xnw[c - 1])
                    t.matmul(ps_b1[:, :], P["ones_f"][0:1, :],
                             stat[0:1, MEAN * CH:(MEAN + 1) * CH], start=True, stop=True)
                    i = t.matmul(ps_b2[:, :], P["ones_f"][0:1, :],
                                 stat[0:1, RST1 * CH:(RST1 + 1) * CH], start=True,
                                 stop=True)
                    t_bc1[c] = ts.mark(i)
                blk.tensor(_tbc1)

                def _vxn(v, c=c, buf=buf):
                    wait(v, t_bc1[c])
                    if c >= 1:
                        wait(v, t_st1[c - 1])
                    for t in range(16):
                        v.tensor_tensor(_t(xn[buf], t, CH), _t(xb[buf], t, CH),
                                        ps_b1[:, :], ALU.subtract)
                    for t in range(16):
                        i = v.tensor_tensor(_t(xn[buf], t, CH), _t(xn[buf], t, CH),
                                            ps_b2[:, :], ALU.mult)
                    t_xn[c] = vs.mark(i)
                blk.vector(_vxn)

                def _ssq2(s, c=c, buf=buf):
                    wait(s, t_xn[c], t_ssum[c])
                    for t in range(16):
                        i = s.activation(_t(sqb, t, CH), _t(xn[buf], t, CH), AF.Square,
                                         bias=0.0, scale=1.0)
                    t_sq2[c] = ss.mark(i)
                blk.scalar(_ssq2)

                def _tsum2(t, c=c):
                    wait(t, t_sq2[c], t_stats[c])
                    for k in range(16):
                        i = t.matmul(ps_s[0:1, :], P["ones_bf"][:, :], _t(sqb, k, CH),
                                     start=(k == 0), stop=(k == 15))
                    t_ssum2[c] = ts.mark(i)
                blk.tensor(_tsum2)

                def _srst2(s, c=c):
                    wait(s, t_ssum2[c])
                    s.activation(stat[0:1, RST2 * CH:(RST2 + 1) * CH], ps_s[0:1, :],
                                 AF.Ln, bias=P["eps"][0:1, :], scale=1.0 / D)
                    i = s.activation(stat[0:1, RST2 * CH:(RST2 + 1) * CH],
                                     stat[0:1, RST2 * CH:(RST2 + 1) * CH], AF.Exp,
                                     bias=0.0, scale=-0.5)
                    t_rst2[c] = ss.mark(i)
                blk.scalar(_srst2)

                def _tbc2(t, c=c):
                    wait(t, t_rst2[c], t_xn[c])
                    i = t.matmul(ps_b1[:, :], P["ones_f"][0:1, :],
                                 stat[0:1, RST2 * CH:(RST2 + 1) * CH], start=True,
                                 stop=True)
                    t_bc2[c] = ts.mark(i)
                blk.tensor(_tbc2)

                def _vxnw(v, c=c, buf=buf):
                    wait(v, t_bc2[c])
                    if c >= 1:
                        wait(v, t_st2[c - 1])
                    for t in range(16):
                        i = v.scalar_tensor_tensor(_t(xnw[buf], t, CH), _t(xn[buf], t, CH),
                                                   P["mnw"][:, t:t + 1], ps_b1[:, :],
                                                   ALU.mult, ALU.mult)
                    t_xnw[c] = vs.mark(i)
                blk.vector(_vxnw)

                def _store(gp, c=c, buf=buf):
                    wait(gp, t_xn[c])
                    t_st1[c] = st1.mark(gp.dma_start(
                        out=g["sc_xn"][:, c * CH:(c + 1) * CH]
                            .rearrange("(t p) c -> p t c", p=128),
                        in_=xn[buf][:].rearrange("p (t c) -> p t c", c=CH)), 16)
                    wait(gp, t_xnw[c])
                    t_st2[c] = st2.mark(gp.dma_start(
                        out=g["sc_xnw"][:, c * CH:(c + 1) * CH]
                            .rearrange("(t p) c -> p t c", p=128),
                        in_=xnw[buf][:].rearrange("p (t c) -> p t c", c=CH)), 16)
                blk.gpsimd(_store)

        b.fence = [vs.tok(), ss.tok(), ts.tok(), ld.tok(), st1.tok(), st2.tok()]


def _phase_a(nc, b, g, P):
    fence = b.fence
    wld = b.sem("a_wld")
    xld = b.sem("a_xld")
    vs = b.sem("a_v")
    ss = b.sem("a_s")
    ts = b.sem("a_t")
    sxs = b.sem("a_sxs")
    ssz = b.sem("a_ssz")
    sdt = b.sem("a_sdt")
    sbc = b.sem("a_sbc")

    import contextlib
    with contextlib.ExitStack() as sk:
        SB = lambda n, sh, dt: sk.enter_context(nc.sbuf_tensor(n, list(sh), dt))
        wpan = [SB(f"a_w{i}", [128, 16 * 1024], BF16) for i in range(2)]
        xnwb = [SB(f"a_xnw{i}", [128, 16 * CH], BF16) for i in range(2)]
        xproj = SB("a_xproj", [128, 32 * 256], BF16)
        dtprj = SB("a_dtprj", [128, 32 * 128], BF16)
        xcbuf = SB("a_xc", [128, CH + 3], F32)
        carry = SB("a_carry", [128, 32 * 3], F32)
        tmp = [SB(f"a_tmp{i}", [128, CH], F32) for i in range(2)]
        xst = [SB(f"a_xs{i}", [128, CH], F32) for i in range(2)]
        xsbf = [SB(f"a_xsbf{i}", [128, CH], BF16) for i in range(2)]
        szt = [SB(f"a_sz{i}", [128, CH], BF16) for i in range(2)]
        dbc_dt = SB("a_dbcdt", [128, NCH * CH], F32)
        dbc_bc = SB("a_dbcbc", [128, NCH * CH], F32)
        dtinb = SB("a_dtin", [128, CH], BF16)
        rbuf = SB("a_r", [128, CH], F32)
        abuf = [SB(f"a_ab{i}", [128, CH], F32) for i in range(2)]
        dtt = [SB(f"a_dtt{i}", [128, CH], F32) for i in range(2)]
        bcout = SB("a_bcout", [128, CH], F32)
        ps_ip = [sk.enter_context(nc.psum_tensor(f"a_psip{i}", [128, CH], F32))
                 for i in range(2)]
        ps_d0 = sk.enter_context(nc.psum_tensor("a_psd0", [128, CH], F32))
        ps_d1 = sk.enter_context(nc.psum_tensor("a_psd1", [128, CH], F32))

        t_wld = [None] * 8
        last_ps_user = [None, None]
        last_xsbf_user = [None, None]
        last_xst_store = [None, None]
        last_szt_store = [None, None]
        last_tmp_user = [None, None]
        last_dbc_evac = None
        last_xnw_user = [None, None]
        last_wpan_user = [None, None]
        t_dbcacc = {}

        with nc.Block() as blk:
            def _w0(gp):
                wait(gp, *fence)
                t0 = wld.mark(gp.dma_start(
                    out=wpan[0][:].rearrange("p (t m) -> p t m", m=1024),
                    in_=g["inproj_T"][:, 0:1024].rearrange("(t p) m -> p t m", p=128)), 16)
                t1 = wld.mark(gp.dma_start(
                    out=xproj[:].rearrange("p (t m) -> p t m", m=256),
                    in_=g["xproj_Tp"][:, :].rearrange("(t p) m -> p t m", p=128)), 16)
                t2 = wld.mark(gp.dma_start(out=dtprj[:], in_=g["dtproj_T"][:, :]), 16)
                return t0, t1, t2
            t_wld[0], t_xpw_ld, t_dtw_ld = em(blk.gpsimd, _w0)

            em(blk.vector, lambda v: (wait(v, *fence), vs.mark(v.memset(carry[:], 0.0)))[1])

            for mg in range(A_MG):
                for c in range(NCH):
                    ib = mg * NCH + c
                    par = ib % 2

                    if c == 0 and mg + 1 < 8:
                        def _wnext(gp, mg=mg):
                            wait(gp, last_wpan_user[(mg + 1) % 2])
                            return wld.mark(gp.dma_start(
                                out=wpan[(mg + 1) % 2][:].rearrange(
                                    "p (t m) -> p t m", m=1024),
                                in_=g["inproj_T"][:, (mg + 1) * 1024:(mg + 2) * 1024]
                                    .rearrange("(t p) m -> p t m", p=128)), 16)
                        t_wld[mg + 1] = em(blk.gpsimd, _wnext)

                    def _xload(s, c=c, ib=ib, par=par):
                        if ib == 0:
                            wait(s, *fence)
                        wait(s, last_xnw_user[par])
                        return xld.mark(s.dma_start(
                            out=xnwb[par][:].rearrange("p (t c) -> p t c", c=CH),
                            in_=g["sc_xnw"][:, c * CH:(c + 1) * CH]
                                .rearrange("(t p) c -> p t c", p=128)), 16)
                    t_xl = em(blk.sync, _xload)

                    for mt in range(8):
                        gdt = mg * 8 + mt
                        pp = mt % 2

                        def _mm(t, mg=mg, par=par, mt=mt, pp=pp, t_xl=t_xl):
                            if mt == 0:
                                wait(t, t_wld[mg], t_xl)
                            wait(t, last_ps_user[pp])
                            for k in range(16):
                                i = t.matmul(ps_ip[pp][:, :],
                                             wpan[mg % 2][:, (k * 8 + mt) * 128:
                                                          (k * 8 + mt) * 128 + 128],
                                             _t(xnwb[par], k, CH),
                                             start=(k == 0), stop=(k == 15))
                            return ts.mark(i)
                        t_mm = em(blk.tensor, _mm)
                        if mt == 7:
                            last_xnw_user[par] = t_mm
                            last_wpan_user[mg % 2] = t_mm

                        if mg < 4:
                            def _conv(v, gdt=gdt, pp=pp, t_mm=t_mm):
                                wait(v, t_mm, last_tmp_user[pp])
                                if not A_CONV:
                                    return vs.mark(v.tensor_copy(tmp[pp][:, :],
                                                                 ps_ip[pp][:, :]))
                                v.tensor_copy(xcbuf[:, 0:3],
                                              carry[:, gdt * 3:gdt * 3 + 3])
                                v.tensor_copy(xcbuf[:, 3:3 + CH], ps_ip[pp][:, :])
                                v.tensor_copy(carry[:, gdt * 3:gdt * 3 + 3],
                                              ps_ip[pp][:, CH - 3:CH])
                                v.tensor_scalar(tmp[pp][:, :], xcbuf[:, 0:CH],
                                                P["convw"][:, gdt * 4:gdt * 4 + 1],
                                                None, ALU.mult)
                                for i4 in range(1, 4):
                                    i = v.scalar_tensor_tensor(
                                        tmp[pp][:, :], xcbuf[:, i4:i4 + CH],
                                        P["convw"][:, gdt * 4 + i4:gdt * 4 + i4 + 1],
                                        tmp[pp][:, :], ALU.mult, ALU.add)
                                return vs.mark(i)
                            t_conv = em(blk.vector, _conv)
                            last_ps_user[pp] = t_conv

                            def _silu(s, gdt=gdt, pp=pp, t_conv=t_conv):
                                wait(s, t_conv, last_xst_store[pp], last_xsbf_user[pp])
                                s.activation(xst[pp][:, :], tmp[pp][:, :], AF.Silu,
                                             bias=P["convb"][:, gdt:gdt + 1], scale=1.0)
                                i = s.activation(xsbf[pp][:, :], tmp[pp][:, :], AF.Silu,
                                                 bias=P["convb"][:, gdt:gdt + 1],
                                                 scale=1.0)
                                return ss.mark(i)
                            t_silu = em(blk.scalar, _silu)
                            last_tmp_user[pp] = t_silu

                            def _xp(t, mg=mg, c=c, mt=mt, gdt=gdt, pp=pp, t_silu=t_silu):
                                wait(t, t_silu)
                                if mt == 0:
                                    wait(t, last_dbc_evac)
                                    if mg == 0 and c == 0:
                                        wait(t, t_xpw_ld)
                                first = (mt == 0)
                                last = (mt == 7)
                                t.matmul(ps_d0[:, :],
                                         xproj[:, gdt * 256:gdt * 256 + 128],
                                         xsbf[pp][:, :], start=first, stop=last)
                                i = t.matmul(ps_d1[:, :],
                                             xproj[:, gdt * 256 + 128:gdt * 256 + 256],
                                             xsbf[pp][:, :], start=first, stop=last)
                                return ts.mark(i)
                            t_xpmm = em(blk.tensor, _xp) if A_XP else t_silu
                            last_xsbf_user[pp] = t_xpmm

                            def _xsst(s2, gdt=gdt, c=c, pp=pp, t_silu=t_silu):
                                wait(s2, t_silu)
                                return sxs.mark(s2.dma_start(
                                    out=g["sc_xs"][gdt * 128:(gdt + 1) * 128,
                                                   c * CH:(c + 1) * CH],
                                    in_=xst[pp][:, :]), 16)
                            last_xst_store[pp] = em(blk.sync, _xsst)

                            if mt == 7 and A_XP:
                                def _dbc(v, mg=mg, c=c, t_xpmm=t_xpmm):
                                    wait(v, t_xpmm)
                                    if mg == 0:
                                        v.tensor_copy(dbc_dt[:, c * CH:(c + 1) * CH],
                                                      ps_d0[:, :])
                                        i = v.tensor_copy(
                                            dbc_bc[:, c * CH:(c + 1) * CH], ps_d1[:, :])
                                    else:
                                        v.tensor_tensor(dbc_dt[:, c * CH:(c + 1) * CH],
                                                        dbc_dt[:, c * CH:(c + 1) * CH],
                                                        ps_d0[:, :], ALU.add)
                                        i = v.tensor_tensor(
                                            dbc_bc[:, c * CH:(c + 1) * CH],
                                            dbc_bc[:, c * CH:(c + 1) * CH],
                                            ps_d1[:, :], ALU.add)
                                    return vs.mark(i)
                                t_dbcacc[(mg, c)] = em(blk.vector, _dbc)
                                last_dbc_evac = t_dbcacc[(mg, c)]
                        else:
                            def _sz(s, pp=pp, t_mm=t_mm):
                                wait(s, t_mm, last_szt_store[pp])
                                return ss.mark(s.activation(szt[pp][:, :],
                                                            ps_ip[pp][:, :], AF.Silu,
                                                            bias=0.0, scale=1.0))
                            t_szdone = em(blk.scalar, _sz)
                            last_ps_user[pp] = t_szdone

                            def _szst(gp, mg=mg, c=c, mt=mt, pp=pp, t_szdone=t_szdone):
                                wait(gp, t_szdone)
                                gdt2 = (mg - 4) * 8 + mt
                                return ssz.mark(gp.dma_start(
                                    out=g["sc_sz"][gdt2 * 128:(gdt2 + 1) * 128,
                                                   c * CH:(c + 1) * CH],
                                    in_=szt[pp][:, :]), 16)
                            last_szt_store[pp] = em(blk.gpsimd, _szst)

            if not A_DT:
                b.fence = [vs.tok(), ss.tok(), ts.tok(), wld.tok(), xld.tok(),
                           sxs.tok(), ssz.tok(), sdt.tok(), sbc.tok()]
                return
            last_ab_user = [None, None]
            last_dt_store = [None, None]
            t_bcst_prev = None
            last_dtmm31 = None
            for c in range(NCH):
                def _vdtin(v, c=c, last_dtmm31=last_dtmm31, t_bcst_prev=t_bcst_prev):
                    wait(v, t_dbcacc[(3, c)], last_dtmm31)
                    tk1 = vs.mark(v.tensor_copy(dtinb[:, :],
                                                dbc_dt[:, c * CH:(c + 1) * CH]))
                    wait(v, t_bcst_prev)
                    v.tensor_tensor(bcout[0:16, :], dbc_bc[0:16, c * CH:(c + 1) * CH],
                                    P["valid"][0:16, c * CH:(c + 1) * CH], ALU.mult)
                    tk2 = vs.mark(v.tensor_copy(bcout[64:80, :],
                                                dbc_bc[64:80, c * CH:(c + 1) * CH]))
                    return tk1, tk2
                t_dtin, t_bcv = em(blk.vector, _vdtin)

                def _bcst(gp, c=c, t_bcv=t_bcv):
                    wait(gp, t_bcv)
                    sbc.mark(gp.dma_start(out=g["sc_bc"][0:16, c * CH:(c + 1) * CH],
                                          in_=bcout[0:16, :]), 16)
                    return sbc.mark(gp.dma_start(
                        out=g["sc_bc"][16:32, c * CH:(c + 1) * CH],
                        in_=bcout[64:80, :]), 16)
                t_bcst_prev = em(blk.gpsimd, _bcst)

                for mt in range(32):
                    pp = mt % 2

                    def _dtmm(t, c=c, mt=mt, pp=pp, t_dtin=t_dtin):
                        if mt == 0:
                            wait(t, t_dtin)
                            if c == 0:
                                wait(t, t_dtw_ld)
                        wait(t, last_ps_user[pp])
                        return ts.mark(t.matmul(ps_ip[pp][:, :],
                                                dtprj[:, mt * 128:(mt + 1) * 128],
                                                dtinb[:, :], start=True, stop=True))
                    t_dtmm = em(blk.tensor, _dtmm)
                    if mt == 31:
                        last_dtmm31 = t_dtmm

                    def _vr(v, mt=mt, pp=pp, t_dtmm=t_dtmm):
                        wait(v, t_dtmm)
                        return vs.mark(v.tensor_scalar(rbuf[:, :], ps_ip[pp][:, :],
                                                       P["dtb"][:, mt:mt + 1], 0.0,
                                                       ALU.add, ALU.max))
                    t_r = em(blk.vector, _vr)

                    def _sab(s, mt=mt, pp=pp, t_dtmm=t_dtmm, t_r=t_r):
                        wait(s, t_dtmm, t_r, last_ab_user[pp])
                        s.activation(abuf[pp][:, :], ps_ip[pp][:, :], AF.Abs,
                                     bias=P["dtb"][:, mt:mt + 1], scale=1.0)
                        s.activation(abuf[pp][:, :], abuf[pp][:, :], AF.Exp,
                                     bias=0.0, scale=-1.0)
                        return ss.mark(s.activation(abuf[pp][:, :], abuf[pp][:, :],
                                                    AF.Ln, bias=1.0, scale=1.0))
                    t_ab = em(blk.scalar, _sab)
                    last_ps_user[pp] = t_ab

                    def _vdt(v, pp=pp, t_ab=t_ab, t_r=t_r):
                        wait(v, t_ab, t_r, last_dt_store[pp])
                        return vs.mark(v.tensor_tensor(dtt[pp][:, :], rbuf[:, :],
                                                       abuf[pp][:, :], ALU.add))
                    t_dtt = em(blk.vector, _vdt)
                    last_ab_user[pp] = t_dtt
                    last_ps_user[pp] = t_dtt

                    def _dtst(s2, c=c, mt=mt, pp=pp, t_dtt=t_dtt):
                        wait(s2, t_dtt)
                        return sdt.mark(s2.dma_start(
                            out=g["sc_dt"][mt * 128:(mt + 1) * 128,
                                           c * CH:(c + 1) * CH],
                            in_=dtt[pp][:, :]), 16)
                    last_dt_store[pp] = em(blk.sync, _dtst)

        b.fence = [vs.tok(), ss.tok(), ts.tok(), wld.tok(), xld.tok(), sxs.tok(),
                   ssz.tok(), sdt.tok(), sbc.tok()]



def _phase_scan(nc, b, g, P):
    fence = b.fence
    ld = b.sem("s_ld")
    bl = b.sem("s_bl")
    zl = b.sem("s_zl")
    vs = b.sem("s_v")
    ss = b.sem("s_s")
    gs = b.sem("s_g")
    so = b.sem("s_so")

    import contextlib
    with contextlib.ExitStack() as sk:
        SB = lambda n, sh, dt: sk.enter_context(nc.sbuf_tensor(n, list(sh), dt))
        xsb = SB("s_xs", [128, 16 * CH], F32)
        dtb = [SB(f"s_dt{i}", [128, 16 * CH], F32) for i in range(2)]
        xdt = SB("s_xdt", [128, 16 * CH], F32)
        szb = [SB(f"s_szt{i}", [128, CH], BF16) for i in range(2)]
        yzb = [SB(f"s_yz{i}", [128, CH], BF16) for i in range(2)]
        dAb = [SB(f"s_dA{i}", [128, CH], F32) for i in range(4)]
        ub = [SB(f"s_u{i}", [128, CH], F32) for i in range(4)]
        sb_ = [SB(f"s_s{i}", [128, CH], F32) for i in range(4)]
        pb = [SB(f"s_p{i}", [128, CH], F32) for i in range(4)]
        brep = [SB(f"s_br{i}", [128, CH], F32) for i in range(2)]
        crep = [SB(f"s_cr{i}", [128, CH], F32) for i in range(2)]

        carry_tok = {}
        t_br = {}
        last_y_done = [None, None]
        prev_iter_yz = None
        last_u_user = [None] * 4
        last_dA_user = [None] * 4
        last_s_user = [None] * 4
        last_p_user = [None] * 4
        last_dt_user = [None, None]
        last_br_user = [None, None]
        last_cr_user = [None, None]
        last_yz_store = [None, None]
        last_sz_user = [None, None]

        with nc.Block() as blk:
            it = -1
            for c in range(NCH):
                for h in range(2):
                    it += 1
                    dpar = it % 2
                    r0 = h * 2048

                    def _ldxs(s, c=c, h=h, it=it, dpar=dpar, r0=r0):
                        if it == 0:
                            wait(s, *fence)
                        wait(s, prev_iter_yz)
                        t1 = ld.mark(s.dma_start(
                            out=xsb[:].rearrange("p (t c) -> p t c", c=CH),
                            in_=g["sc_xs"][r0:r0 + 2048, c * CH:(c + 1) * CH]
                                .rearrange("(t p) c -> p t c", p=128)), 16)
                        wait(s, last_dt_user[dpar])
                        t2 = ld.mark(s.dma_start(
                            out=dtb[dpar][:].rearrange("p (t c) -> p t c", c=CH),
                            in_=g["sc_dt"][r0:r0 + 2048, c * CH:(c + 1) * CH]
                                .rearrange("(t p) c -> p t c", p=128)), 16)
                        return t1, t2
                    t_xs, t_dt = em(blk.sync, _ldxs)

                    for n0 in range(2):
                        def _ldbc(s, c=c, it=it, n0=n0):
                            wait(s, last_br_user[n0], last_cr_user[n0])
                            src_b = bass.AP(g["sc_bc"].tensor, n0 * SEQ + c * CH,
                                            [[0, 128], [1, CH]])
                            src_c = bass.AP(g["sc_bc"].tensor, (16 + n0) * SEQ + c * CH,
                                            [[0, 128], [1, CH]])
                            bl.mark(s.dma_start(out=brep[n0][:], in_=src_b), 16)
                            return bl.mark(s.dma_start(out=crep[n0][:], in_=src_c), 16)
                        t_br[(it, n0)] = em(blk.sync, _ldbc)

                    def _vpre2(v, t_xs=t_xs, t_dt=t_dt, dpar=dpar, h=h):
                        wait(v, t_xs, t_dt)
                        wait(v, last_u_user[0], last_u_user[1])
                        for lt in range(16):
                            v.tensor_tensor(_t(xdt, lt, CH), _t(xsb, lt, CH),
                                            _t(dtb[dpar], lt, CH), ALU.mult)
                        for lt in range(16):
                            gd = h * 16 + lt
                            i = v.tensor_scalar(_t(xsb, lt, CH), _t(xsb, lt, CH),
                                                P["Dpar"][:, gd:gd + 1], None, ALU.mult)
                        return vs.mark(i)
                    t_pre = em(blk.vector, _vpre2)

                    t_gadd_last = [None] * 16
                    for n in range(NST):
                        npar = n % 2
                        for lt in range(16):
                            gd = h * 16 + lt
                            wpar = lt % 4
                            col = gd * NST + n

                            def _sdA(s, n=n, lt=lt, gd=gd, wpar=wpar, dpar=dpar,
                                     t_dt=t_dt):
                                if n == 0 and lt == 0:
                                    wait(s, t_dt)
                                wait(s, last_dA_user[wpar])
                                return ss.mark(s.activation(
                                    dAb[wpar][:, :], _t(dtb[dpar], lt, CH), AF.Exp,
                                    bias=0.0, scale=P["Aneg"][:, col:col + 1]))
                            t_dA = em(blk.scalar, _sdA)

                            def _vus(v, n=n, lt=lt, wpar=wpar, npar=npar, col=col,
                                     t_dA=t_dA, t_pre=t_pre, it=it, c=c):
                                if lt == 0:
                                    wait(v, t_br[(it, n)])
                                if n == 0 and lt == 0:
                                    wait(v, t_pre)
                                wait(v, last_u_user[wpar])
                                v.tensor_tensor(ub[wpar][:, :], _t(xdt, lt, CH),
                                                brep[npar][:, :], ALU.mult)
                                wait(v, t_dA, last_s_user[wpar])
                                if c == 0:
                                    init = 0.0
                                else:
                                    wait(v, carry_tok.get((col,)))
                                    init = P["carry"][:, col:col + 1]
                                i = v.tensor_tensor_scan(
                                    sb_[wpar][:, :], dAb[wpar][:, :], ub[wpar][:, :],
                                    init, ALU.mult, ALU.add)
                                return vs.mark(i)
                            t_scan = em(blk.vector, _vus)
                            last_dA_user[wpar] = t_scan
                            last_u_user[wpar] = t_scan

                            def _vp(v, n=n, lt=lt, wpar=wpar, npar=npar,
                                    t_scan=t_scan, it=it):
                                wait(v, last_p_user[wpar])
                                i = v.tensor_tensor(pb[wpar][:, :], sb_[wpar][:, :],
                                                    crep[npar][:, :], ALU.mult)
                                return vs.mark(i)
                            t_prod = em(blk.vector, _vp)
                            if S_DBG == (it, lt):
                                def _dbgst(s2, n=n, wpar=wpar, t_scan=t_scan,
                                           t_prod=t_prod, t_dA=t_dA):
                                    wait(s2, t_prod)
                                    so.mark(s2.dma_start(
                                        out=g["sc_dbg"][:, n * CH:(n + 1) * CH],
                                        in_=sb_[wpar][:, :]), 16)
                                    so.mark(s2.dma_start(
                                        out=g["sc_dbg"][:, (NST + n) * CH:
                                                        (NST + n + 1) * CH],
                                        in_=dAb[wpar][:, :]), 16)
                                    return so.mark(s2.dma_start(
                                        out=g["sc_dbg"][:, (2 * NST + n) * CH:
                                                        (2 * NST + n + 1) * CH],
                                        in_=ub[wpar][:, :]), 16)
                                em(blk.sync, _dbgst)
                            last_cr_user[npar] = t_prod
                            last_br_user[npar] = t_prod

                            def _ga(gp, n=n, lt=lt, wpar=wpar, t_prod=t_prod,
                                    t_pre=t_pre):
                                wait(gp, t_prod)
                                if n == 0:
                                    wait(gp, t_pre)
                                return gs.mark(gp.tensor_tensor(
                                    _t(xsb, lt, CH), _t(xsb, lt, CH), pb[wpar][:, :],
                                    ALU.add))
                            t_ga = em(blk.gpsimd, _ga)
                            t_gadd_last[lt] = t_ga
                            last_p_user[wpar] = t_ga

                            def _scc(s, col=col, wpar=wpar, t_ga=t_ga):
                                wait(s, t_ga)
                                return ss.mark(s.activation(
                                    P["carry"][:, col:col + 1],
                                    sb_[wpar][:, CH - 1:CH], AF.Identity,
                                    bias=0.0, scale=1.0))
                            carry_tok[(col,)] = em(blk.scalar, _scc)
                            last_s_user[wpar] = carry_tok[(col,)]

                        if n + 2 < NST:
                            def _ldbc2(s, c=c, n=n, npar=npar,
                                       t_last=last_br_user[npar]):
                                wait(s, t_last)
                                src_b = bass.AP(g["sc_bc"].tensor,
                                                (n + 2) * SEQ + c * CH,
                                                [[0, 128], [1, CH]])
                                src_c = bass.AP(g["sc_bc"].tensor,
                                                (16 + n + 2) * SEQ + c * CH,
                                                [[0, 128], [1, CH]])
                                bl.mark(s.dma_start(out=brep[npar][:], in_=src_b), 16)
                                return bl.mark(s.dma_start(out=crep[npar][:],
                                                           in_=src_c), 16)
                            t_br[(it, n + 2)] = em(blk.sync, _ldbc2)
                    last_dt_user[dpar] = t_dA

                    t_yz_iter = None
                    for lt in range(16):
                        zpar = lt % 2
                        gd = h * 16 + lt

                        def _lz(gp, c=c, gd=gd, zpar=zpar, lt=lt):
                            wait(gp, last_sz_user[zpar])
                            return zl.mark(gp.dma_start(
                                out=szb[zpar][:, :],
                                in_=g["sc_sz"][gd * 128:(gd + 1) * 128,
                                               c * CH:(c + 1) * CH]), 16)
                        t_z = em(blk.gpsimd, _lz)

                        def _vyz(gp, lt=lt, zpar=zpar, t_z=t_z,
                                 t_g=t_gadd_last[lt]):
                            wait(gp, t_z, t_g, last_yz_store[zpar])
                            return gs.mark(gp.tensor_tensor(
                                yzb[zpar][:, :], _t(xsb, lt, CH), szb[zpar][:, :],
                                ALU.mult))
                        t_yz = em(blk.gpsimd, _vyz)
                        last_sz_user[zpar] = t_yz
                        t_yz_iter = t_yz

                        def _syz(s, c=c, gd=gd, zpar=zpar, t_yz=t_yz):
                            wait(s, t_yz)
                            return so.mark(s.dma_start(
                                out=g["sc_yz"][gd * 128:(gd + 1) * 128,
                                               c * CH:(c + 1) * CH],
                                in_=yzb[zpar][:, :]), 16)
                        last_yz_store[zpar] = em(blk.sync, _syz)
                    prev_iter_yz = t_yz_iter

        b.fence = [vs.tok(), ss.tok(), gs.tok(), ld.tok(), bl.tok(), zl.tok(),
                   so.tok()]


def _phase_m(nc, b, g, P):
    fence = b.fence
    yld = b.sem("m_yld")
    xld = b.sem("m_xld")
    wl = b.sem("m_wl")
    vs = b.sem("m_v")
    ss = b.sem("m_s")
    ts = b.sem("m_t")
    sk_ = b.sem("m_sk")
    sv_ = b.sem("m_sv")
    sq_ = b.sem("m_sq")
    sh_ = b.sem("m_sh")

    import contextlib
    with contextlib.ExitStack() as sk:
        SB = lambda n, sh, dt: sk.enter_context(nc.sbuf_tensor(n, list(sh), dt))
        yzb = SB("m_yz", [128, 32 * CH], BF16)
        hbuf = SB("m_h", [128, 16 * CH], F32)
        hnwb = SB("m_hnw", [128, 16 * CH], BF16)
        sqb = SB("m_sq", [128, 16 * CH], BF16)
        wbuf = [SB(f"m_w{i}", [128, 32 * 128], BF16) for i in range(2)]
        wqb = [SB(f"m_wq{i}", [128, 16 * 128], BF16) for i in range(2)]
        xnb = [SB(f"m_xn{i}", [128, CH], F32) for i in range(2)]
        ktl = [SB(f"m_kt{i}", [128, CH], BF16) for i in range(4)]
        vtm = [SB(f"m_vt{i}", [128, CH], BF16) for i in range(2)]
        vte = [SB(f"m_vte{i}", [128, CH], BF16) for i in range(2)]
        qtl = [SB(f"m_qt{i}", [128, CH], BF16) for i in range(4)]
        stat = SB("m_stat", [1, CH], F32)
        ps_a = [sk.enter_context(nc.psum_tensor(f"m_ps{i}", [128, CH], F32))
                for i in range(2)]
        ps_st = sk.enter_context(nc.psum_tensor("m_psst", [128, CH], F32))
        ps_bc = sk.enter_context(nc.psum_tensor("m_psbc", [128, CH], F32))
        ps_tr = sk.enter_context(nc.psum_tensor("m_pstr", [128, 128], BF16))

        last_ps = [None, None]
        last_w = [None, None]
        last_wq = [None, None]
        last_xn = [None, None]
        last_kt = [None] * 4
        last_vtm = [None, None]
        last_vte = [None, None]
        last_qt = [None] * 4
        last_tr = None
        prev_hnw_done = None
        prev_h_store = None
        prev_yz_user = None
        b.kvdone = {}

        with nc.Block() as blk:
            for c in range(NCH):
                def _ldyz(s, c=c):
                    if c == 0:
                        wait(s, *fence)
                    wait(s, prev_yz_user)
                    return yld.mark(s.dma_start(
                        out=yzb[:].rearrange("p (t c) -> p t c", c=CH),
                        in_=g["sc_yz"][:, c * CH:(c + 1) * CH]
                            .rearrange("(t p) c -> p t c", p=128)), 16)
                t_yz = em(blk.sync, _ldyz)

                t_mm_last = None
                for mt in range(16):
                    pp = mt % 2

                    def _wl(gp, c=c, mt=mt, pp=pp):
                        if c == 0 and mt == 0:
                            wait(gp, *fence)
                        wait(gp, last_w[pp])
                        return wl.mark(gp.dma_start(
                            out=wbuf[pp][:].rearrange("p (t m) -> p t m", m=128),
                            in_=g["mambaout_T"][:, mt * 128:(mt + 1) * 128]
                                .rearrange("(t p) m -> p t m", p=128)), 16)
                    t_w = em(blk.gpsimd, _wl)

                    def _ldxn(s, c=c, mt=mt, pp=pp):
                        wait(s, last_xn[pp])
                        return xld.mark(s.dma_start(
                            out=xnb[pp][:, :],
                            in_=g["sc_xn"][mt * 128:(mt + 1) * 128,
                                           c * CH:(c + 1) * CH]), 16)
                    t_xn = em(blk.sync, _ldxn)

                    def _mm(t, mt=mt, pp=pp, t_w=t_w, t_yz=t_yz):
                        wait(t, t_w, last_ps[pp])
                        if mt == 0:
                            wait(t, t_yz)
                        for k in range(32):
                            i = t.matmul(ps_a[pp][:, :],
                                         wbuf[pp][:, k * 128:(k + 1) * 128],
                                         _t(yzb, k, CH), start=(k == 0), stop=(k == 31))
                        return ts.mark(i)
                    t_mm = em(blk.tensor, _mm)
                    last_w[pp] = t_mm
                    t_mm_last = t_mm

                    def _hev(v, mt=mt, pp=pp, t_mm=t_mm, t_xn=t_xn,
                             ph=prev_hnw_done, phs=prev_h_store):
                        wait(v, t_mm, t_xn)
                        if mt < 2:
                            wait(v, ph, phs)
                        return vs.mark(v.tensor_tensor(_t(hbuf, mt, CH),
                                                       ps_a[pp][:, :], xnb[pp][:, :],
                                                       ALU.add))
                    t_h = em(blk.vector, _hev)
                    last_ps[pp] = t_h
                    last_xn[pp] = t_h

                    def _sq(s, mt=mt, t_h=t_h):
                        return ss.mark(s.activation(_t(sqb, mt, CH), _t(hbuf, mt, CH),
                                                    AF.Square, bias=0.0, scale=1.0,
                                                    ))
                    def _sqw(s, mt=mt, t_h=t_h, _f=_sq):
                        wait(s, t_h)
                        return _f(s)
                    t_sq = em(blk.scalar, _sqw)
                    if mt == 0:
                        t_sq0 = t_sq
                    t_sq_last = t_sq

                def _msum(t, t_sq_last=t_sq_last):
                    wait(t, t_sq_last)
                    for k in range(16):
                        i = t.matmul(ps_st[0:1, :], P["ones_bf"][:, :], _t(sqb, k, CH),
                                     start=(k == 0), stop=(k == 15))
                    return ts.mark(i)
                t_ms = em(blk.tensor, _msum)

                def _rstd(s, t_ms=t_ms):
                    wait(s, t_ms)
                    s.activation(stat[0:1, :], ps_st[0:1, :], AF.Ln,
                                 bias=P["eps"][0:1, :], scale=1.0 / D)
                    return ss.mark(s.activation(stat[0:1, :], stat[0:1, :], AF.Exp,
                                                bias=0.0, scale=-0.5))
                t_rs = em(blk.scalar, _rstd)

                def _bc(t, t_rs=t_rs):
                    wait(t, t_rs)
                    return ts.mark(t.matmul(ps_bc[:, :], P["ones_f"][0:1, :],
                                            stat[0:1, :], start=True, stop=True))
                t_bc = em(blk.tensor, _bc)

                def _hnw(v, t_bc=t_bc):
                    wait(v, t_bc)
                    for mt in range(16):
                        i = v.scalar_tensor_tensor(_t(hnwb, mt, CH), _t(hbuf, mt, CH),
                                                   P["anw"][:, mt:mt + 1], ps_bc[:, :],
                                                   ALU.mult, ALU.mult)
                    return vs.mark(i)
                t_hnw = em(blk.vector, _hnw)
                prev_hnw_done = t_hnw

                kv_toks = []
                for j in range(8):
                    pp = j % 2
                    mcol = 2048 + j * 128

                    def _wql(gp, mcol=mcol, pp=pp):
                        wait(gp, last_wq[pp])
                        return wl.mark(gp.dma_start(
                            out=wqb[pp][:].rearrange("p (t m) -> p t m", m=128),
                            in_=g["wqkv_T"][:, mcol:mcol + 128]
                                .rearrange("(t p) m -> p t m", p=128)), 16)
                    t_wq = em(blk.gpsimd, _wql)

                    def _qmm(t, pp=pp, t_wq=t_wq, t_hnw=t_hnw):
                        wait(t, t_wq, last_ps[pp], t_hnw)
                        for k in range(16):
                            i = t.matmul(ps_a[pp][:, :],
                                         wqb[pp][:, k * 128:(k + 1) * 128],
                                         _t(hnwb, k, CH), start=(k == 0), stop=(k == 15))
                        return ts.mark(i)
                    t_qmm = em(blk.tensor, _qmm)
                    last_wq[pp] = t_qmm

                    if j < 4:
                        def _kev(v, j=j, pp=pp, c=c, t_qmm=t_qmm):
                            wait(v, t_qmm, last_kt[j])
                            bcol = 16 + j
                            return vs.mark(v.scalar_tensor_tensor(
                                ktl[j][:, :], ps_a[pp][:, :],
                                P["qkvb"][:, bcol:bcol + 1],
                                P["valid"][:, c * CH:(c + 1) * CH],
                                ALU.add, ALU.mult))
                        t_k = em(blk.vector, _kev)
                        last_ps[pp] = t_k

                        def _kst(s, j=j, c=c, t_k=t_k):
                            wait(s, t_k)
                            return sk_.mark(s.dma_start(
                                out=g["sc_kc"][j * 128:(j + 1) * 128,
                                               c * CH:(c + 1) * CH],
                                in_=ktl[j][:, :]), 16)
                        last_kt[j] = em(blk.sync, _kst)
                        kv_toks.append(last_kt[j])
                    else:
                        jj = j - 4
                        def _vev(v, jj=jj, pp=pp, c=c, t_qmm=t_qmm):
                            wait(v, t_qmm, last_vtm[jj % 2])
                            bcol = 20 + jj
                            return vs.mark(v.scalar_tensor_tensor(
                                vtm[jj % 2][:, :], ps_a[pp][:, :],
                                P["qkvb"][:, bcol:bcol + 1],
                                P["valid"][:, c * CH:(c + 1) * CH],
                                ALU.add, ALU.mult))
                        t_v = em(blk.vector, _vev)
                        last_ps[pp] = t_v

                        for st in range(4):
                            def _tr(t, jj=jj, st=st, t_v=t_v):
                                wait(t, t_v, last_tr)
                                return ts.mark(t.transpose(
                                    ps_tr[:, :], vtm[jj % 2][:, st * 128:(st + 1) * 128],
                                    P["ident"][:, :]))
                            t_t = em(blk.tensor, _tr)

                            def _trev(v, st=st, t_t=t_t):
                                wait(v, t_t, last_vte[st % 2])
                                return vs.mark(v.tensor_copy(
                                    vte[st % 2][:, 0:128], ps_tr[:, :]))
                            t_te = em(blk.vector, _trev)
                            last_tr = t_te

                            def _vst(s, jj=jj, st=st, c=c, t_te=t_te):
                                wait(s, t_te)
                                col = (jj * 32 + c * 4 + st) * 128
                                return sv_.mark(s.dma_start(
                                    out=g["sc_vtc"][:, col:col + 128],
                                    in_=vte[st % 2][:, 0:128]), 16)
                            last_vte[st % 2] = em(blk.sync, _vst)
                            kv_toks.append(last_vte[st % 2])
                        last_vtm[jj % 2] = last_tr

                if c >= PCH0:
                    for j in range(16):
                        pp = j % 2
                        def _wql2(gp, j=j, pp=pp):
                            wait(gp, last_wq[pp])
                            return wl.mark(gp.dma_start(
                                out=wqb[pp][:].rearrange("p (t m) -> p t m", m=128),
                                in_=g["wqkv_T"][:, j * 128:(j + 1) * 128]
                                    .rearrange("(t p) m -> p t m", p=128)), 16)
                        t_wq = em(blk.gpsimd, _wql2)

                        def _qmm2(t, pp=pp, t_wq=t_wq, t_hnw=t_hnw):
                            wait(t, t_wq, last_ps[pp], t_hnw)
                            for k in range(16):
                                i = t.matmul(ps_a[pp][:, :],
                                             wqb[pp][:, k * 128:(k + 1) * 128],
                                             _t(hnwb, k, CH), start=(k == 0),
                                             stop=(k == 15))
                            return ts.mark(i)
                        t_qm = em(blk.tensor, _qmm2)
                        last_wq[pp] = t_qm

                        def _qev(v, j=j, pp=pp, t_qm=t_qm):
                            wait(v, t_qm, last_qt[j % 4])
                            return vs.mark(v.tensor_scalar(
                                qtl[j % 4][:, :], ps_a[pp][:, :],
                                P["qkvb"][:, j:j + 1], None, ALU.add))
                        t_q = em(blk.vector, _qev)
                        last_ps[pp] = t_q

                        def _qst(s, j=j, c=c, t_q=t_q):
                            wait(s, t_q)
                            return sq_.mark(s.dma_start(
                                out=g["sc_q"][j * 128:(j + 1) * 128,
                                              (c - PCH0) * CH:(c - PCH0 + 1) * CH],
                                in_=qtl[j % 4][:, :]), 16)
                        last_qt[j % 4] = em(blk.sync, _qst)

                    def _hst(gp, c=c, t_hnw=t_hnw):
                        wait(gp, t_hnw)
                        return sh_.mark(gp.dma_start(
                            out=g["sc_h"][:, (c - PCH0) * CH:(c - PCH0 + 1) * CH]
                                .rearrange("(t p) c -> p t c", p=128),
                            in_=hbuf[:].rearrange("p (t c) -> p t c", c=CH)), 16)
                    prev_h_store = em(blk.gpsimd, _hst)
                prev_yz_user = t_mm_last
                b.kvdone[c] = kv_toks + [prev_hnw_done]

        b.fence = [vs.tok(), ss.tok(), ts.tok(), yld.tok(), xld.tok(), wl.tok(),
                   sk_.tok(), sv_.tok(), sq_.tok(), sh_.tok()]


def _phase_att(nc, b, g, P):
    fence = b.fence
    ql = b.sem("t_ql")
    kl = b.sem("t_kl")
    vl = b.sem("t_vl")
    hl = b.sem("t_hl")
    wl = b.sem("t_wl")
    vs = b.sem("t_v")
    ss = b.sem("t_s")
    ts = b.sem("t_t")
    s2 = b.sem("t_s2")
    s3 = b.sem("t_s3")

    NKT = 20

    import contextlib
    with contextlib.ExitStack() as sk:
        SB = lambda n, sh, dt: sk.enter_context(nc.sbuf_tensor(n, list(sh), dt))
        qb = SB("t_q", [128, 16 * CH], BF16)
        kb = SB("t_k", [128, 4 * 2560], BF16)
        vtb = SB("t_vt", [128, 4 * NKT * 128], BF16)
        ptc = SB("t_ptc", [128, NKT * CH], BF16)
        Pb = SB("t_P", [128, KW], BF16)
        ab4 = SB("t_a4", [128, 4 * CH], F32)
        zq = SB("t_zq", [128, 2 * 256], F32)
        zz = SB("t_zz", [128, 256], F32)
        zn = [SB(f"t_zn{i}", [128, 256], BF16) for i in range(2)]
        acb = SB("t_ac", [128, 16 * CH], BF16)
        h2b = SB("t_h2", [128, 16 * CH], F32)
        h2nw = [SB(f"t_h2nw{i}", [128, CH], BF16) for i in range(2)]
        sqb = SB("t_sqb", [128, 16 * CH], BF16)
        htl = [SB(f"t_htl{i}", [128, CH], F32) for i in range(2)]
        opwb = [SB(f"t_opwb{i}", [128, 16 * 128], BF16) for i in range(2)]
        lbuf = SB("t_lb", [128, 16], F32)
        mxb = SB("t_mx", [128, 8], F32)
        junk = SB("t_junk", [128, CH], F32)
        lamc = SB("t_lam", [128, 16], F32)
        stat = SB("t_st", [1, CH], F32)
        ps_S = sk.enter_context(nc.psum_tensor("t_psS", [128, KW], F32))
        ps_T4 = sk.enter_context(nc.psum_tensor("t_psT4", [128, 4 * 128], BF16))
        ps_U = [sk.enter_context(nc.psum_tensor(f"t_psU{i}", [128, CH], F32))
                for i in range(2)]
        ps_O = ps_U
        ps_st = ps_S

        last = {}

        def L(key):
            return last.get(key)

        def S(key, tok):
            last[key] = tok
            return tok

        with nc.Block() as blk:
            prev_chunk_end = list(fence)
            for c in range(PCH0):
                pc = c + PCH0
                kwin0 = pc * CH - WINDOW

                def _lq(s, c=c):
                    if c == 0:
                        wait(s, *fence)
                    wait(s, L("qb"))
                    return ql.mark(s.dma_start(
                        out=qb[:].rearrange("p (t c) -> p t c", c=CH),
                        in_=g["sc_q"][:, c * CH:(c + 1) * CH]
                            .rearrange("(t p) c -> p t c", p=128)), 16)
                t_q = em(blk.sync, _lq)

                def _lk(s, c=c, pc=pc, kwin0=kwin0):
                    wait(s, L("kb"))
                    if c == 0:
                        for tk in b.kvdone.get(pc, []):
                            wait(s, tk)
                    else:
                        for tk in b.kvdone.get(pc, []):
                            wait(s, tk)
                    return kl.mark(s.dma_start(
                        out=kb[:].rearrange("p (t c) -> p t c", c=2560),
                        in_=g["sc_kc"][:, kwin0:kwin0 + 2560]
                            .rearrange("(t p) c -> p t c", p=128)), 16)
                t_k = em(blk.sync, _lk)

                def _lv(s, c=c, kwin0=kwin0):
                    wait(s, L("vtb"))
                    st0 = kwin0 // 128
                    i = None
                    for hd_ in range(4):
                        i = vl.mark(s.dma_start(
                            out=vtb[:, hd_ * NKT * 128:(hd_ + 1) * NKT * 128],
                            in_=g["sc_vtc"][:, (hd_ * 32 + st0) * 128:
                                            (hd_ * 32 + st0 + NKT) * 128]), 16)
                    return i
                t_v = em(blk.sync, _lv)

                t_ac_tiles = []
                for j in range(8):
                    lam_tok = None
                    for si in range(2):
                        kh = 2 * (j // 4) + si
                        qtile = 2 * j + si
                        def _msp(v, j=j, si=si):
                            wait(v, L("ptc_pv"))
                            return vs.mark(v.memset(ptc[:], 0.0))
                        t_msp = em(blk.vector, _msp)

                        for qt in range(4):
                            qtA = pc * 4 + qt

                            def _sc(t, j=j, si=si, qt=qt, kh=kh, qtile=qtile,
                                    t_q=t_q, t_k=t_k):
                                wait(t, t_q, t_k, L("psS"))
                                for i5 in range(5):
                                    w = 512 if i5 < 4 else 128
                                    i = t.matmul(
                                        ps_S[:, i5 * 512:i5 * 512 + w],
                                        qb[:, qtile * CH + qt * 128:
                                           qtile * CH + qt * 128 + 128],
                                        kb[:, kh * 2560 + qt * 128 + i5 * 512:
                                           kh * 2560 + qt * 128 + i5 * 512 + w],
                                        start=True, stop=True)
                                return ts.mark(i)
                            t_sc = em(blk.tensor, _sc)

                            def _msk(v, qt=qt, t_sc=t_sc):
                                wait(v, t_sc, L("mx_exp"))
                                v.tensor_tensor(ps_S[:, 0:128], ps_S[:, 0:128],
                                                P["bnd"][:, 128:256], ALU.add)
                                v.tensor_tensor(ps_S[:, 2048:2176], ps_S[:, 2048:2176],
                                                P["bnd"][:, 0:128], ALU.add)
                                v.reduce_max(mxb[:, 0:1], ps_S[:, :],
                                             axis=mybir.AxisListType.X)
                                v.tensor_copy(junk[:, :], ps_S[:, 0:CH])
                                v.tensor_scalar(mxb[:, 1:2], mxb[:, 0:1],
                                                -SCL, None, ALU.mult)
                                i = v.tensor_copy(junk[:, :], ps_S[:, 512:512 + CH])
                                return vs.mark(i)
                            t_mx = em(blk.vector, _msk)

                            def _exp(s, si=si, qt=qt, qtA=qtA, t_mx=t_mx):
                                wait(s, t_mx, L("Pb"))
                                s.activation(lbuf[:, 8 + si * 4 + qt:
                                                  9 + si * 4 + qt],
                                             mxb[:, 1:2], AF.Exp, bias=0.0, scale=1.0)
                                i = s.activation(Pb[:, :], ps_S[:, :], AF.Exp,
                                                 bias=mxb[:, 1:2], scale=SCL,
                                                 accum_out=lbuf[:, si * 4 + qt:
                                                                si * 4 + qt + 1])
                                return ss.mark(i)
                            t_e = em(blk.scalar, _exp)
                            S("psS", t_e)
                            S("mx_exp", t_e)
                            if T_DBG3 == (c, j, si, qt):
                                def _dpb(gp, t_e=t_e):
                                    wait(gp, t_e)
                                    s3.mark(gp.dma_start(
                                        out=g["sc_dbg"][:, 20 * CH:20 * CH + KW],
                                        in_=Pb[:, :]), 16)
                                    return s3.mark(gp.dma_start(
                                        out=g["sc_dbg"][:, 23 * CH:23 * CH + 8],
                                        in_=mxb[:, :]), 16)
                                em(blk.gpsimd, _dpb)

                            for grp in range(5):
                                n_in = 4 if grp < 4 else 1
                                def _tr(t, grp=grp, n_in=n_in, t_e=t_e):
                                    wait(t, t_e, L("psT4"))
                                    for u_ in range(n_in):
                                        ktl_ = grp * 4 + u_
                                        i = t.transpose(
                                            ps_T4[:, u_ * 128:(u_ + 1) * 128],
                                            Pb[:, ktl_ * 128:(ktl_ + 1) * 128],
                                            P["ident"][:, :])
                                    return ts.mark(i)
                                t_tr = em(blk.tensor, _tr)
                                if grp == 4:
                                    S("Pb", t_tr)

                                def _trev(v, grp=grp, n_in=n_in, qt=qt, t_tr=t_tr,
                                          t_msp=t_msp):
                                    wait(v, t_tr, t_msp)
                                    i = None
                                    for u_ in range(n_in):
                                        ktw = qt + grp * 4 + u_
                                        i = v.tensor_copy(
                                            ptc[:, ktw * CH + qt * 128:
                                                ktw * CH + qt * 128 + 128],
                                            ps_T4[:, u_ * 128:(u_ + 1) * 128])
                                    return vs.mark(i)
                                S("psT4", em(blk.vector, _trev))
                        t_ptc_done = last["psT4"]

                        vh = [2 * (j // 4), 2 * (j // 4) + 1]
                        def _pv(t, si=si, vh=vh, t_ptc_done=t_ptc_done, t_v=t_v):
                            wait(t, t_ptc_done, t_v, L("psU0"), L("psU1"), L("psF"))
                            for kt in range(NKT):
                                for u_ in range(2):
                                    i = t.matmul(
                                        ps_U[u_][:, :],
                                        vtb[:, (vh[u_] * NKT + kt) * 128:
                                            (vh[u_] * NKT + kt) * 128 + 128],
                                        ptc[:, kt * CH:(kt + 1) * CH],
                                        start=(kt == 0), stop=(kt == NKT - 1))
                            return ts.mark(i)
                        t_pv = em(blk.tensor, _pv)
                        S("ptc_pv", t_pv)

                        def _aev(v, si=si, t_pv=t_pv):
                            wait(v, t_pv)
                            v.tensor_copy(ab4[:, (2 * si) * CH:(2 * si + 1) * CH],
                                          ps_U[0][:, :])
                            i = v.tensor_copy(ab4[:, (2 * si + 1) * CH:
                                                  (2 * si + 2) * CH], ps_U[1][:, :])
                            return vs.mark(i)
                        t_ae = em(blk.vector, _aev)
                        S("psU0", t_ae)
                        S("psU1", t_ae)

                    if T_DBG2 == (c, j):
                        def _dab(gp, t_ae=t_ae):
                            wait(gp, t_ae)
                            return s3.mark(gp.dma_start(
                                out=g["sc_dbg"][:, 16 * CH:20 * CH],
                                in_=ab4[:]), 16)
                        em(blk.gpsimd, _dab)

                    def _lam4(s, pc=pc):
                        wait(s, L("lamc"))
                        for qt in range(4):
                            s.activation(lamc[:, qt * 4:qt * 4 + 1],
                                         lbuf[:, 8 + qt:9 + qt], AF.Relu,
                                         bias=lbuf[:, qt:qt + 1],
                                         scale=P["npad"][:, pc * 4 + qt - 16:
                                                         pc * 4 + qt - 15])
                        for qt in range(4):
                            s.activation(lamc[:, qt * 4 + 1:qt * 4 + 2],
                                         lbuf[:, 12 + qt:13 + qt], AF.Relu,
                                         bias=lbuf[:, 4 + qt:5 + qt],
                                         scale=P["npad"][:, pc * 4 + qt - 16:
                                                         pc * 4 + qt - 15])
                        for qt in range(4):
                            s.activation(lamc[:, qt * 4:qt * 4 + 1],
                                         lamc[:, qt * 4:qt * 4 + 1], AF.Ln,
                                         bias=0.0, scale=1.0)
                        for qt in range(4):
                            s.activation(lamc[:, qt * 4 + 1:qt * 4 + 2],
                                         lamc[:, qt * 4 + 1:qt * 4 + 2], AF.Ln,
                                         bias=0.0, scale=1.0)
                        for qt in range(4):
                            s.activation(lamc[:, qt * 4 + 2:qt * 4 + 3],
                                         lamc[:, qt * 4 + 1:qt * 4 + 2], AF.Exp,
                                         bias=lamc[:, qt * 4:qt * 4 + 1],
                                         scale=-1.0)
                        for qt in range(4):
                            s.activation(lamc[:, qt * 4 + 3:qt * 4 + 4],
                                         lamc[:, qt * 4 + 2:qt * 4 + 3], AF.Copy,
                                         bias=0.0, scale=P["lamn"][:, :])
                        return ss.mark(s.activation(junk[:, 0:256], junk[:, 0:256],
                                                    AF.Copy, bias=0.0, scale=1.0))
                    t_lam = em(blk.scalar, _lam4)

                    for qt in range(4):
                        for u_ in range(4):
                            def _trf(t, u_=u_, qt=qt, t_ae=t_ae):
                                wait(t, t_ae, L("psF"))
                                return ts.mark(t.transpose(
                                    ps_U[0][:, 0:128],
                                    ab4[:, u_ * CH + qt * 128:u_ * CH + qt * 128 + 128],
                                    P["ident_f"][:, :]))
                            t_tf = em(blk.tensor, _trf)

                            def _tfe(v, u_=u_, t_tf=t_tf):
                                wait(v, t_tf, L("zq") if u_ < 2 else None)
                                col = (u_ % 2) * 128
                                dst = zq[:, (u_ // 2) * 256 + col:
                                         (u_ // 2) * 256 + col + 128]
                                return vs.mark(v.tensor_copy(dst, ps_U[0][:, 0:128]))
                            S("psF", em(blk.vector, _tfe))
                        t_zq = last["psF"]

                        def _zz(v, qt=qt, t_zq=t_zq, t_lam=t_lam):
                            wait(v, t_zq, t_lam, L("zz"))
                            return vs.mark(v.scalar_tensor_tensor(
                                zz[:, :], zq[:, 256:512],
                                lamc[:, qt * 4 + 3:qt * 4 + 4], zq[:, 0:256],
                                ALU.mult, ALU.add))
                        t_z = em(blk.vector, _zz)
                        S("zq", t_z)
                        S("lamc", t_z)

                        def _ms(s, t_z=t_z):
                            i = s.activation(sqb[:, 0:256], zz[:, :], AF.Square,
                                             bias=0.0, scale=1.0,
                                             accum_out=mxb[:, 3:4])
                            s.activation(mxb[:, 4:5], mxb[:, 3:4], AF.Ln,
                                         bias=P["eps"][:, :], scale=1.0 / 256)
                            i = s.activation(mxb[:, 4:5], mxb[:, 4:5], AF.Exp,
                                             bias=0.0, scale=-0.5)
                            return ss.mark(i)
                        def _msw(s, _f=_ms, t_z=t_z):
                            wait(s, t_z, L("sqz"))
                            return _f(s)
                        t_rs = em(blk.scalar, _msw)

                        def _zn(s, qt=qt, t_rs=t_rs):
                            wait(s, L(("zn", qt % 2)))
                            return ss.mark(s.activation(
                                zn[qt % 2][:, :], zz[:, :], AF.Copy, bias=0.0,
                                scale=mxb[:, 4:5]))
                        t_znt = em(blk.scalar, _zn)
                        S("zz", t_znt)
                        S("sqz", t_znt)

                        if T_DBG4 == (c, j, qt):
                            def _dz(gp, t_znt=t_znt):
                                wait(gp, t_znt)
                                s3.mark(gp.dma_start(
                                    out=g["sc_dbg"][:, 0:512], in_=zq[:, :]), 16)
                                s3.mark(gp.dma_start(
                                    out=g["sc_dbg"][:, 512:768], in_=zz[:, :]), 16)
                                s3.mark(gp.dma_start(
                                    out=g["sc_dbg"][:, 1024:1040], in_=lamc[:, :]), 16)
                                return s3.mark(gp.dma_start(
                                    out=g["sc_dbg"][:, 1056:1072], in_=lbuf[:, :]), 16)
                            t_dmp = em(blk.gpsimd, _dz)
                            for ky in ("zq", "zz", "sqz", "lamc", "mx_exp"):
                                S(ky, t_dmp)

                        for u_ in range(2):
                            def _trb(t, u_=u_, qt=qt, t_znt=t_znt):
                                wait(t, t_znt, L("psT4"))
                                return ts.mark(t.transpose(
                                    ps_T4[:, u_ * 128:(u_ + 1) * 128],
                                    zn[qt % 2][:, u_ * 128:(u_ + 1) * 128],
                                    P["ident"][:, :]))
                            t_tb = em(blk.tensor, _trb)

                            def _tbe(v, j=j, u_=u_, qt=qt, t_tb=t_tb):
                                wait(v, t_tb, L("acbc"))
                                dst = acb[:, (2 * j + u_) * CH + qt * 128:
                                          (2 * j + u_) * CH + qt * 128 + 128]
                                return vs.mark(v.tensor_copy(
                                    dst, ps_T4[:, u_ * 128:(u_ + 1) * 128]))
                            S("psT4", em(blk.vector, _tbe))
                        S(("zn", qt % 2), last["psT4"])
                t_acb_done = last["psT4"]

                if T_DBG == c:
                    def _dacb(gp, t_acb=t_acb_done):
                        wait(gp, t_acb)
                        return s3.mark(gp.dma_start(
                            out=g["sc_dbg"][:, 0:16 * CH],
                            in_=acb[:]), 16)
                    em(blk.gpsimd, _dacb)

                for mt in range(16):
                    pp = mt % 2

                    def _wld(gp, mt=mt, pp=pp, c=c):
                        wait(gp, L(("opwb", pp)))
                        return wl.mark(gp.dma_start(
                            out=opwb[pp][:].rearrange("p (t m) -> p t m", m=128),
                            in_=g["outproj_T"][:, mt * 128:(mt + 1) * 128]
                                .rearrange("(t p) m -> p t m", p=128)), 16)
                    t_wo = em(blk.gpsimd, _wld)

                    def _wsc(s, pp=pp, t_wo=t_wo):
                        wait(s, t_wo)
                        i = None
                        for kt in range(16):
                            i = s.activation(opwb[pp][:, kt * 128:(kt + 1) * 128],
                                             opwb[pp][:, kt * 128:(kt + 1) * 128],
                                             AF.Copy, bias=0.0,
                                             scale=P["subln"][:, kt % 2:kt % 2 + 1])
                        return ss.mark(i)
                    t_ws = em(blk.scalar, _wsc)

                    def _lh(s, mt=mt, pp=pp, c=c):
                        wait(s, L(("htl", pp)))
                        return hl.mark(s.dma_start(
                            out=htl[pp][:, :],
                            in_=g["sc_h"][mt * 128:(mt + 1) * 128,
                                          c * CH:(c + 1) * CH]), 16)
                    t_h = em(blk.sync, _lh)

                    def _omm(t, mt=mt, pp=pp, t_ws=t_ws, t_acb=t_acb_done):
                        wait(t, t_ws, t_acb, L(("psO", pp)))
                        if mt < 2:
                            wait(t, L("psU0"), L("psU1"))
                        for k in range(16):
                            i = t.matmul(ps_O[pp][:, :],
                                         opwb[pp][:, k * 128:(k + 1) * 128],
                                         _t(acb, k, CH), start=(k == 0), stop=(k == 15))
                        return ts.mark(i)
                    t_om = em(blk.tensor, _omm)
                    S(("opwb", pp), t_om)
                    if mt == 15:
                        S("acbc", t_om)

                    def _h2(v, mt=mt, pp=pp, t_om=t_om, t_h=t_h):
                        wait(v, t_om, t_h)
                        if mt < 2:
                            wait(v, L("h2nw_done"), L("h2_store"), L("h2nw_store"))
                        i = v.scalar_tensor_tensor(_t(h2b, mt, CH), ps_O[pp][:, :],
                                                   P["opb"][:, mt:mt + 1],
                                                   htl[pp][:, :], ALU.add, ALU.add)
                        return vs.mark(i)
                    t_h2 = em(blk.vector, _h2)
                    S(("psO", pp), t_h2)
                    S(("htl", pp), t_h2)

                    def _sq3(s, mt=mt, t_h2=t_h2):
                        wait(s, t_h2)
                        return ss.mark(s.activation(_t(sqb, mt, CH), _t(h2b, mt, CH),
                                                    AF.Square, bias=0.0, scale=1.0))
                    t_s3 = em(blk.scalar, _sq3)
                    if mt == 15:
                        t_sq3_last = t_s3

                def _msum3(t, t_sq3_last=t_sq3_last):
                    wait(t, t_sq3_last, L("ps_st"), L("psS"))
                    for k in range(16):
                        i = t.matmul(ps_st[0:1, 0:CH], P["ones_bf"][:, :],
                                     _t(sqb, k, CH), start=(k == 0), stop=(k == 15))
                    return ts.mark(i)
                t_ms3 = em(blk.tensor, _msum3)

                def _rstd3(s, t_ms3=t_ms3):
                    wait(s, t_ms3)
                    s.activation(stat[0:1, :], ps_st[0:1, 0:CH], AF.Ln,
                                 bias=P["eps"][0:1, :], scale=1.0 / D)
                    return ss.mark(s.activation(stat[0:1, :], stat[0:1, :], AF.Exp,
                                                bias=0.0, scale=-0.5))
                t_rs3 = em(blk.scalar, _rstd3)

                def _bc3(t, t_rs3=t_rs3):
                    wait(t, t_rs3)
                    return ts.mark(t.matmul(ps_st[0:128, 0:CH], P["ones_f"][0:1, :],
                                            stat[0:1, :], start=True, stop=True))
                t_bc3 = em(blk.tensor, _bc3)
                S("ps_st", t_bc3)

                t_hnw_last = None
                for mt in range(16):
                    def _hnw3(v, mt=mt, t_bc3=t_bc3):
                        if mt == 0:
                            wait(v, t_bc3)
                        wait(v, L(("h2nws", mt % 2)))
                        return vs.mark(v.scalar_tensor_tensor(
                            h2nw[mt % 2][:, :], _t(h2b, mt, CH),
                            P["pnw"][:, mt:mt + 1], ps_st[0:128, 0:CH],
                            ALU.mult, ALU.mult))
                    t_hh = em(blk.vector, _hnw3)
                    t_hnw_last = t_hh

                    def _sth2n(gp, c=c, mt=mt, t_hh=t_hh):
                        wait(gp, t_hh)
                        return s3.mark(gp.dma_start(
                            out=g["sc_h2nw"][mt * 128:(mt + 1) * 128,
                                             c * CH:(c + 1) * CH],
                            in_=h2nw[mt % 2][:, :]), 16)
                    S(("h2nws", mt % 2), em(blk.gpsimd, _sth2n))
                t_hnw3 = t_hnw_last
                S("h2nw_done", t_hnw3)
                S("ps_st", t_hnw3)
                S("psS", t_hnw3)
                S("h2nw_store", last[("h2nws", 1)])

                def _sth2(s, c=c, t_hnw3=t_hnw3):
                    wait(s, t_hnw3)
                    return s2.mark(s.dma_start(
                        out=g["sc_h2"][:, c * CH:(c + 1) * CH]
                            .rearrange("(t p) c -> p t c", p=128),
                        in_=h2b[:].rearrange("p (t c) -> p t c", c=CH)), 16)
                S("h2_store", em(blk.sync, _sth2))
                S("qb", t_acb_done)
                S("kb", t_acb_done)
                S("vtb", t_acb_done)

        b.fence = [vs.tok(), ss.tok(), ts.tok(), ql.tok(), kl.tok(), vl.tok(),
                   hl.tok(), wl.tok(), s2.tok(), s3.tok()]


def _phase_mlp(nc, b, g, P):
    fence = b.fence
    hl = b.sem("mphl")
    wl = b.sem("mpwl")
    gl = b.sem("mpgl")
    xl = b.sem("mpxl")
    vs = b.sem("mpv")
    ss = b.sem("mps")
    ts = b.sem("mpt")
    s2 = b.sem("mps2")
    s3 = b.sem("mps3")
    import contextlib
    last = {}

    def L(key):
        return last.get(key)

    def S(key, tok):
        last[key] = tok
        return tok

    st1 = contextlib.ExitStack()
    SB1 = lambda n, sh, dt: st1.enter_context(nc.sbuf_tensor(n, list(sh), dt))
    hb = SB1("mp_h", [128, 4 * 16 * CH], BF16)
    wg = [SB1(f"mp_wg{i}", [128, 16 * 128], BF16) for i in range(2)]
    wu = [SB1(f"mp_wu{i}", [128, 16 * 128], BF16) for i in range(2)]
    sgl = [SB1(f"mp_sg{i}", [128, CH], F32) for i in range(2)]
    glb = [SB1(f"mp_gl{i}", [128, CH], BF16) for i in range(2)]
    ps_g = [st1.enter_context(nc.psum_tensor(f"mp_psg{i}", [128, CH], F32))
            for i in range(2)]
    ps_u = [st1.enter_context(nc.psum_tensor(f"mp_psu{i}", [128, CH], F32))
            for i in range(2)]

    with nc.Block() as blk:
        for c in range(4):
            def _lh(s, c=c):
                if c == 0:
                    wait(s, *fence)
                return hl.mark(s.dma_start(
                    out=hb[:, c * 16 * CH:(c + 1) * 16 * CH]
                        .rearrange("p (t x) -> p t x", x=CH),
                    in_=g["sc_h2nw"][:, c * CH:(c + 1) * CH]
                        .rearrange("(t p) x -> p t x", p=128)), 16)
            S(("hb", c), em(blk.sync, _lh))

        for gm in range(64):
            pp = gm % 2

            def _lw(gp, gm=gm, pp=pp):
                if gm < 2:
                    wait(gp, *fence)
                wait(gp, L(("wg", pp)))
                wl.mark(gp.dma_start(
                    out=wg[pp][:].rearrange("p (t m) -> p t m", m=128),
                    in_=g["fc1_T"][:, gm * 128:(gm + 1) * 128]
                        .rearrange("(t p) m -> p t m", p=128)), 16)
                return wl.mark(gp.dma_start(
                    out=wu[pp][:].rearrange("p (t m) -> p t m", m=128),
                    in_=g["fc1_T"][:, INTER + gm * 128:INTER + (gm + 1) * 128]
                        .rearrange("(t p) m -> p t m", p=128)), 16)
            t_w = em(blk.gpsimd, _lw)

            for c in range(4):
                qq = c % 2

                def _mm(t, pp=pp, qq=qq, c=c, t_w=t_w):
                    wait(t, t_w, L(("hb", c)), L(("psg", qq)), L(("psu", qq)))
                    for kt in range(16):
                        t.matmul(ps_g[qq][:, :],
                                 wg[pp][:, kt * 128:(kt + 1) * 128],
                                 hb[:, (c * 16 + kt) * CH:(c * 16 + kt + 1) * CH],
                                 start=(kt == 0), stop=(kt == 15))
                    for kt in range(16):
                        i = t.matmul(ps_u[qq][:, :],
                                     wu[pp][:, kt * 128:(kt + 1) * 128],
                                     hb[:, (c * 16 + kt) * CH:(c * 16 + kt + 1) * CH],
                                     start=(kt == 0), stop=(kt == 15))
                    return ts.mark(i)
                t_mm = em(blk.tensor, _mm)
                if c == 3:
                    S(("wg", pp), t_mm)

                def _sil(s, qq=qq, t_mm=t_mm):
                    wait(s, t_mm, L(("sgl", qq)))
                    return ss.mark(s.activation(sgl[qq][:, :], ps_g[qq][:, :],
                                                AF.Silu, bias=0.0, scale=1.0))
                t_si = em(blk.scalar, _sil)
                S(("psg", qq), t_si)

                def _glu(v, qq=qq, t_si=t_si):
                    wait(v, t_si, L(("glb", qq)))
                    return vs.mark(v.tensor_tensor(glb[qq][:, :], ps_u[qq][:, :],
                                                   sgl[qq][:, :], ALU.mult))
                t_gl = em(blk.vector, _glu)
                S(("psu", qq), t_gl)
                S(("sgl", qq), t_gl)

                def _st(s, gm=gm, qq=qq, c=c, t_gl=t_gl):
                    wait(s, t_gl)
                    return s2.mark(s.dma_start(
                        out=g["sc_glu"][gm * 128:(gm + 1) * 128,
                                        c * CH:(c + 1) * CH],
                        in_=glb[qq][:, :]), 16)
                S(("glb", qq), em(blk.sync, _st))

    fence1 = [vs.tok(), ss.tok(), ts.tok(), hl.tok(), wl.tok(), s2.tok()]
    st1.close()

    st2 = contextlib.ExitStack()
    SB2 = lambda n, sh, dt: st2.enter_context(nc.sbuf_tensor(n, list(sh), dt))
    f2acc = SB2("mp_acc", [128, 8 * 4 * CH], F32)
    w8b = SB2("mp_w8", [128, 8 * 1024], BF16)
    gb8 = SB2("mp_g8", [128, 8 * POST0], BF16)
    xt2 = [SB2(f"mp_xt2{i}", [128, CH], F32) for i in range(2)]
    h2t2 = [SB2(f"mp_h2t2{i}", [128, CH], F32) for i in range(2)]
    r1t = [SB2(f"mp_r1t{i}", [128, CH], F32) for i in range(2)]
    ps_f = [st2.enter_context(nc.psum_tensor(f"mp_psf{i}", [128, CH], F32))
            for i in range(2)]

    with nc.Block() as blk:
        for mh in range(2):
            for k8 in range(8):
                def _lw8(gp, mh=mh, k8=k8):
                    if mh == 0 and k8 == 0:
                        wait(gp, *fence1)
                    else:
                        wait(gp, L("mm_all"))
                    return wl.mark(gp.dma_start(
                        out=w8b[:].rearrange("p (t m) -> p t m", m=1024),
                        in_=g["fc2_T"][k8 * 1024:(k8 + 1) * 1024,
                                       mh * 1024:(mh + 1) * 1024]
                            .rearrange("(t p) m -> p t m", p=128)), 16)
                t_w8 = em(blk.gpsimd, _lw8)

                def _lg8(s, mh=mh, k8=k8):
                    if mh == 0 and k8 == 0:
                        wait(s, *fence1)
                    else:
                        wait(s, L("mm_all"))
                    return gl.mark(s.dma_start(
                        out=gb8[:].rearrange("p (t x) -> p t x", x=POST0),
                        in_=g["sc_glu"][k8 * 1024:(k8 + 1) * 1024, :]
                            .rearrange("(t p) x -> p t x", p=128)), 16)
                t_g8 = em(blk.sync, _lg8)

                t_last_mm = None
                for mt8 in range(8):
                    for c in range(4):
                        qq = (mt8 * 4 + c) % 2

                        def _fm(t, mt8=mt8, c=c, qq=qq, t_w8=t_w8, t_g8=t_g8):
                            wait(t, t_w8, t_g8, L(("psf", qq)))
                            for kt in range(8):
                                i = t.matmul(
                                    ps_f[qq][:, :],
                                    w8b[:, kt * 1024 + mt8 * 128:
                                        kt * 1024 + (mt8 + 1) * 128],
                                    gb8[:, kt * POST0 + c * CH:
                                        kt * POST0 + (c + 1) * CH],
                                    start=(kt == 0), stop=(kt == 7))
                            return ts.mark(i)
                        t_fm = em(blk.tensor, _fm)
                        t_last_mm = t_fm

                        def _acc(v, k8=k8, mt8=mt8, c=c, qq=qq, t_fm=t_fm):
                            wait(v, t_fm)
                            sl = f2acc[:, (mt8 * 4 + c) * CH:
                                       (mt8 * 4 + c + 1) * CH]
                            if k8 == 0:
                                i = v.tensor_copy(sl, ps_f[qq][:, :])
                            else:
                                i = v.tensor_tensor(sl, sl, ps_f[qq][:, :],
                                                    ALU.add)
                            return vs.mark(i)
                        S(("psf", qq), em(blk.vector, _acc))
                S("mm_all", t_last_mm)

            for mt8 in range(8):
                mt = mh * 8 + mt8
                for c in range(4):
                    pp = (mt8 * 4 + c) % 2

                    def _lx2(s, mt=mt, mt8=mt8, pp=pp, c=c, mh=mh):
                        if mh == 0 and mt8 * 4 + c < 2:
                            wait(s, *fence1)
                        wait(s, L(("xt2", pp)))
                        xl.mark(s.dma_start(
                            out=xt2[pp][:, :],
                            in_=g["x_in"][mt * 128:(mt + 1) * 128,
                                          POST0 + c * CH:
                                          POST0 + (c + 1) * CH]), 16)
                        wait(s, L(("h2t2", pp)))
                        return xl.mark(s.dma_start(
                            out=h2t2[pp][:, :],
                            in_=g["sc_h2"][mt * 128:(mt + 1) * 128,
                                           c * CH:(c + 1) * CH]), 16)
                    t_lx = em(blk.sync, _lx2)

                    def _r1(v, mt8=mt8, pp=pp, c=c, t_lx=t_lx):
                        wait(v, t_lx, L(("r1t", pp)))
                        v.tensor_tensor(r1t[pp][:, :], xt2[pp][:, :],
                                        h2t2[pp][:, :], ALU.add)
                        i = v.tensor_tensor(r1t[pp][:, :], r1t[pp][:, :],
                                            f2acc[:, (mt8 * 4 + c) * CH:
                                                  (mt8 * 4 + c + 1) * CH],
                                            ALU.add)
                        return vs.mark(i)
                    t_r1 = em(blk.vector, _r1)
                    S(("xt2", pp), t_r1)
                    S(("h2t2", pp), t_r1)

                    def _sr1(gp, mt=mt, pp=pp, c=c, t_r1=t_r1):
                        wait(gp, t_r1)
                        return s3.mark(gp.dma_start(
                            out=g["sc_r1"][mt * 128:(mt + 1) * 128,
                                           c * CH:(c + 1) * CH],
                            in_=r1t[pp][:, :]), 16)
                    S(("r1t", pp), em(blk.gpsimd, _sr1))

    fence2 = [vs.tok(), ts.tok(), wl.tok(), gl.tok(), xl.tok(), s3.tok()]
    st2.close()

    st3 = contextlib.ExitStack()
    SB3 = lambda n, sh, dt: st3.enter_context(nc.sbuf_tensor(n, list(sh), dt))
    r1b = SB3("mp_r1", [128, 16 * CH], F32)
    sqb = SB3("mp_sq", [128, 16 * CH], BF16)
    x2b = [SB3(f"mp_x2{i}", [128, CH], BF16) for i in range(2)]
    stat = SB3("mp_stat", [1, 4 * CH], F32)
    rtw = SB3("mp_rt", [128, 16 * NEXP], F32)
    sgs = SB3("mp_sgs", [128, 4 * 128], F32)
    scT = SB3("mp_scT", [128, 64], F32)
    sB = SB3("mp_sB", [128, 64], F32)
    pairS = SB3("mp_pair", [128, 6 * 16], F32)
    gsb = SB3("mp_gs", [128, 16], F32)
    dmn = SB3("mp_dmn", [128, 6 * 4], F32)
    t2g = SB3("mp_t2g", [128, 8], F32)
    gmask = SB3("mp_gm", [128, 16], F32)
    mbuf = SB3("mp_mb", [128, 64], F32)
    tr1 = SB3("mp_tr1", [128, 64], F32)
    tr2 = SB3("mp_tr2", [128, 64], F32)
    selb = SB3("mp_sel", [128, 64], F32)
    wgt = SB3("mp_wgt", [128, 64], F32)
    wsum = SB3("mp_ws", [128, 8], F32)
    comb = SB3("mp_comb", [128, 64], F32)
    combrow = SB3("mp_cr", [16, 4 * 128], F32)
    jk = SB3("mp_jk", [128, 256], F32)
    ps_m = st3.enter_context(nc.psum_tensor("mp_psm", [128, CH], F32))
    ps_s = st3.enter_context(nc.psum_tensor("mp_pss", [128, CH], F32))
    ps_b1 = st3.enter_context(nc.psum_tensor("mp_psb1", [128, CH], F32))
    ps_b2 = st3.enter_context(nc.psum_tensor("mp_psb2", [128, CH], F32))
    ps_r = st3.enter_context(nc.psum_tensor("mp_psr", [128, CH], F32))
    ps_t = st3.enter_context(nc.psum_tensor("mp_pst", [128, 128], F32))
    MEAN, VAR, TMP, RST = 0, 1, 2, 3

    with nc.Block() as blk:
        def _lrt(s):
            wait(s, *fence2)
            return xl.mark(s.dma_start(
                out=rtw[:].rearrange("p (t e) -> p t e", e=NEXP),
                in_=g["router_T"][:, :].rearrange("(t p) e -> p t e", p=128)), 16)
        t_rt = em(blk.sync, _lrt)

        def _ini(v):
            wait(v, *fence2)
            return vs.mark(v.memset(sgs[:], 0.0))
        t_ini = em(blk.vector, _ini)

        for c in range(4):
            for mt in range(16):
                def _lr(s, mt=mt, c=c):
                    if c == 0 and mt == 0:
                        wait(s, *fence2)
                    if mt == 0 and c > 0:
                        wait(s, L("r1b_free"), L("cx_last"))
                    return xl.mark(s.dma_start(
                        out=_t(r1b, mt, CH),
                        in_=g["sc_r1"][mt * 128:(mt + 1) * 128,
                                       c * CH:(c + 1) * CH]), 16)
                t_r1 = em(blk.sync, _lr)

                def _sq(s, mt=mt, t_r1=t_r1, c=c):
                    wait(s, t_r1)
                    if c > 0 and mt == 0:
                        wait(s, L("sq_free"))
                    return ss.mark(s.activation(_t(sqb, mt, CH), _t(r1b, mt, CH),
                                                AF.Square, bias=0.0, scale=1.0))
                t_sq = em(blk.scalar, _sq)
                if mt == 15:
                    t_sq_last = t_sq
                    t_r1_last = t_r1

            def _tsum(t, t_r1_last=t_r1_last, t_sq_last=t_sq_last):
                wait(t, t_r1_last, L("psm"))
                for k in range(16):
                    i = t.matmul(ps_m[0:1, :], P["ones_fc"][:, :], _t(r1b, k, CH),
                                 start=(k == 0), stop=(k == 15))
                t_m = ts.mark(i)
                wait(t, t_sq_last, L("pss"))
                for k in range(16):
                    i = t.matmul(ps_s[0:1, :], P["ones_bf"][:, :], _t(sqb, k, CH),
                                 start=(k == 0), stop=(k == 15))
                return t_m, ts.mark(i)
            t_msum, t_ssum = em(blk.tensor, _tsum)
            S("sq_free", t_ssum)

            def _vstat(v, t_msum=t_msum, t_ssum=t_ssum):
                wait(v, t_msum, t_ssum)
                v.tensor_scalar(stat[0:1, MEAN * CH:(MEAN + 1) * CH], ps_m[0:1, :],
                                1.0 / D, None, ALU.mult)
                v.tensor_scalar(stat[0:1, VAR * CH:(VAR + 1) * CH], ps_s[0:1, :],
                                1.0 / D, None, ALU.mult)
                v.tensor_tensor(stat[0:1, TMP * CH:(TMP + 1) * CH],
                                stat[0:1, MEAN * CH:(MEAN + 1) * CH],
                                stat[0:1, MEAN * CH:(MEAN + 1) * CH], ALU.mult)
                i = v.tensor_tensor(stat[0:1, VAR * CH:(VAR + 1) * CH],
                                    stat[0:1, VAR * CH:(VAR + 1) * CH],
                                    stat[0:1, TMP * CH:(TMP + 1) * CH],
                                    ALU.subtract)
                return vs.mark(i)
            t_stat = em(blk.vector, _vstat)
            S("psm", t_stat)
            S("pss", t_stat)

            def _srst(s, t_stat=t_stat):
                wait(s, t_stat)
                s.activation(stat[0:1, RST * CH:(RST + 1) * CH],
                             stat[0:1, VAR * CH:(VAR + 1) * CH], AF.Ln,
                             bias=P["eps"][0:1, :], scale=1.0)
                i = s.activation(stat[0:1, RST * CH:(RST + 1) * CH],
                                 stat[0:1, RST * CH:(RST + 1) * CH], AF.Exp,
                                 bias=0.0, scale=-0.5)
                return ss.mark(i)
            t_rst = em(blk.scalar, _srst)

            def _tbc(t, t_rst=t_rst):
                wait(t, t_rst, L("psb"))
                t.matmul(ps_b1[:, :], P["ones_f"][0:1, :],
                         stat[0:1, MEAN * CH:(MEAN + 1) * CH], start=True,
                         stop=True)
                i = t.matmul(ps_b2[:, :], P["ones_f"][0:1, :],
                             stat[0:1, RST * CH:(RST + 1) * CH], start=True,
                             stop=True)
                return ts.mark(i)
            t_bc = em(blk.tensor, _tbc)

            t_xe = [None] * 16
            for mt in range(16):
                def _vxe(v, mt=mt, t_bc=t_bc):
                    wait(v, t_bc)
                    v.tensor_tensor(_t(r1b, mt, CH), _t(r1b, mt, CH), ps_b1[:, :],
                                    ALU.subtract)
                    i = v.tensor_tensor(_t(r1b, mt, CH), _t(r1b, mt, CH),
                                        ps_b2[:, :], ALU.mult)
                    return vs.mark(i)
                t_xe[mt] = em(blk.vector, _vxe)
            S("psb", t_xe[15])

            for mt in range(16):
                pp = mt % 2

                def _cx(s, mt=mt, pp=pp, t_x=t_xe[mt]):
                    wait(s, t_x, L(("x2b", pp)))
                    return ss.mark(s.activation(x2b[pp][:, :], _t(r1b, mt, CH),
                                                AF.Copy, bias=0.0, scale=1.0))
                t_cx = em(blk.scalar, _cx)
                if mt == 15:
                    S("cx_last", t_cx)

                def _sx(gp, mt=mt, pp=pp, c=c, t_cx=t_cx):
                    wait(gp, t_cx)
                    return s3.mark(gp.dma_start(
                        out=g["sc_x2w"][mt * 128:(mt + 1) * 128,
                                        c * CH:(c + 1) * CH],
                        in_=x2b[pp][:, :]), 16)
                S(("x2b", pp), em(blk.gpsimd, _sx))

            def _rmm(t, t_rt=t_rt, t_xe=t_xe):
                wait(t, t_rt, t_xe[15], L("psr"))
                for kt in range(16):
                    i = t.matmul(ps_r[0:16, :], rtw[:, kt * NEXP:(kt + 1) * NEXP],
                                 _t(r1b, kt, CH), start=(kt == 0), stop=(kt == 15))
                return ts.mark(i)
            t_rm = em(blk.tensor, _rmm)
            S("r1b_free", t_rm)

            def _sig(s, t_rm=t_rm, t_ini=t_ini):
                wait(s, t_rm, t_ini, L("sgs"))
                return ss.mark(s.activation(sgs[0:16, :], ps_r[0:16, :],
                                            AF.Sigmoid, bias=0.0, scale=1.0))
            t_sg = em(blk.scalar, _sig)
            S("psr", t_sg)

            for sub in range(4):
                def _tt(t, sub=sub, t_sg=t_sg):
                    wait(t, t_sg, L("pst"))
                    return ts.mark(t.transpose(
                        ps_t[:, :], sgs[:, sub * 128:(sub + 1) * 128],
                        P["ident_f"][:, :]))
                t_tp = em(blk.tensor, _tt)

                def _ev(v, sub=sub, t_tp=t_tp, c=c):
                    wait(v, t_tp)
                    if sub == 0 and c > 0:
                        wait(v, L("scT_free"))
                    return vs.mark(v.tensor_copy(scT[:, sub * 16:(sub + 1) * 16],
                                                 ps_t[:, 0:16]))
                S("pst", em(blk.vector, _ev))
            t_sct = last["pst"]
            S("sgs", t_sct)

            def _route(v, t_sct=t_sct):
                wait(v, t_sct)

                def pad():
                    v.tensor_copy(jk[:, :], sgs[:, 0:256])
                    v.tensor_copy(jk[:, :], sgs[:, 0:256])

                for sub in range(4):
                    v.tensor_tensor(sB[:, sub * 16:(sub + 1) * 16],
                                    scT[:, sub * 16:(sub + 1) * 16],
                                    P["ebias"][:, :], ALU.add)
                pad()
                sv = sB[:, 0:64].rearrange("p (x i) -> p x i", i=4)
                for pi, (i_, j_) in enumerate([(0, 1), (0, 2), (0, 3),
                                               (1, 2), (1, 3), (2, 3)]):
                    v.tensor_tensor(
                        pairS[:, pi * 16:(pi + 1) * 16]
                        .rearrange("p (x o) -> p x o", o=1),
                        sv[:, :, i_:i_ + 1], sv[:, :, j_:j_ + 1], ALU.add)
                pad()
                v.tensor_tensor(tr1[:, 0:16], pairS[:, 0:16], pairS[:, 16:32],
                                ALU.max)
                v.tensor_tensor(tr1[:, 16:32], pairS[:, 32:48], pairS[:, 48:64],
                                ALU.max)
                v.tensor_tensor(tr1[:, 32:48], pairS[:, 64:80], pairS[:, 80:96],
                                ALU.max)
                v.tensor_tensor(tr1[:, 48:64], tr1[:, 0:16], tr1[:, 16:32],
                                ALU.max)
                pad()
                v.tensor_tensor(gsb[:, 0:16], tr1[:, 48:64], tr1[:, 32:48],
                                ALU.max)
                pad()
                gv = gsb[:, 0:16].rearrange("p (s gr) -> p s gr", gr=4)
                for pi, (i_, j_) in enumerate([(0, 1), (0, 2), (0, 3),
                                               (1, 2), (1, 3), (2, 3)]):
                    v.tensor_tensor(
                        dmn[:, pi * 4:(pi + 1) * 4]
                        .rearrange("p (s o) -> p s o", o=1),
                        gv[:, :, i_:i_ + 1], gv[:, :, j_:j_ + 1], ALU.min)
                pad()
                v.tensor_tensor(t2g[:, 0:4], dmn[:, 0:4], dmn[:, 4:8], ALU.max)
                v.tensor_tensor(t2g[:, 4:8], dmn[:, 8:12], dmn[:, 12:16], ALU.max)
                v.tensor_tensor(dmn[:, 0:4], dmn[:, 16:20], dmn[:, 20:24], ALU.max)
                v.tensor_tensor(t2g[:, 0:4], t2g[:, 0:4], t2g[:, 4:8], ALU.max)
                pad()
                v.tensor_tensor(t2g[:, 0:4], t2g[:, 0:4], dmn[:, 0:4], ALU.max)
                pad()
                for sub in range(4):
                    v.tensor_scalar(gmask[:, sub * 4:(sub + 1) * 4],
                                    gsb[:, sub * 4:(sub + 1) * 4],
                                    t2g[:, sub:sub + 1], None, ALU.is_ge)
                pad()
                for sub in range(4):
                    for gr in range(4):
                        v.tensor_scalar(
                            mbuf[:, sub * 16 + gr * 4:sub * 16 + (gr + 1) * 4],
                            sB[:, sub * 16 + gr * 4:sub * 16 + (gr + 1) * 4],
                            gmask[:, sub * 4 + gr:sub * 4 + gr + 1], None,
                            ALU.mult)
                pad()
                mv = mbuf[:, 0:64].rearrange("p (x two) -> p x two", two=2)
                v.tensor_tensor(tr1[:, 0:32].rearrange("p (x o) -> p x o", o=1),
                                mv[:, :, 0:1], mv[:, :, 1:2], ALU.max)
                v.tensor_tensor(tr2[:, 0:32].rearrange("p (x o) -> p x o", o=1),
                                mv[:, :, 0:1], mv[:, :, 1:2], ALU.min)
                pad()
                for (n_, w_) in ((32, 16), (16, 8), (8, 4)):
                    Mv = tr1[:, 0:n_].rearrange("p (x two) -> p x two", two=2)
                    Sv = tr2[:, 0:n_].rearrange("p (x two) -> p x two", two=2)
                    v.tensor_tensor(tr1[:, 32:32 + w_]
                                    .rearrange("p (x o) -> p x o", o=1),
                                    Mv[:, :, 0:1], Mv[:, :, 1:2], ALU.max)
                    v.tensor_tensor(tr2[:, 32:32 + w_]
                                    .rearrange("p (x o) -> p x o", o=1),
                                    Mv[:, :, 0:1], Mv[:, :, 1:2], ALU.min)
                    v.tensor_tensor(tr2[:, 48:48 + w_]
                                    .rearrange("p (x o) -> p x o", o=1),
                                    Sv[:, :, 0:1], Sv[:, :, 1:2], ALU.max)
                    pad()
                    v.tensor_tensor(tr2[:, 32:32 + w_], tr2[:, 32:32 + w_],
                                    tr2[:, 48:48 + w_], ALU.max)
                    pad()
                    v.tensor_copy(tr1[:, 0:w_], tr1[:, 32:32 + w_])
                    v.tensor_copy(tr2[:, 0:w_], tr2[:, 32:32 + w_])
                    pad()
                for sub in range(4):
                    v.tensor_scalar(selb[:, sub * 16:(sub + 1) * 16],
                                    mbuf[:, sub * 16:(sub + 1) * 16],
                                    tr2[:, sub:sub + 1], None, ALU.is_ge)
                pad()
                v.tensor_tensor(wgt[:, 0:64], selb[:, 0:64], scT[:, 0:64],
                                ALU.mult)
                pad()
                for sub in range(4):
                    v.tensor_scalar(jk[:, 0:16], wgt[:, sub * 16:(sub + 1) * 16],
                                    0.0, None, ALU.add, op1=ALU.add,
                                    accum_out=wsum[:, sub:sub + 1])
                pad()
                v.tensor_scalar(wsum[:, 4:8], wsum[:, 0:4], 1e-20, None, ALU.add)
                pad()
                v.reciprocal(wsum[:, 4:8], wsum[:, 4:8])
                pad()
                for sub in range(4):
                    v.tensor_scalar(comb[:, sub * 16:(sub + 1) * 16],
                                    wgt[:, sub * 16:(sub + 1) * 16],
                                    wsum[:, 4 + sub:5 + sub], None, ALU.mult)
                i = v.tensor_copy(jk[:, :], sgs[:, 0:256])
                return vs.mark(i)
            t_route = em(blk.vector, _route)
            S("scT_free", t_route)

            for sub in range(4):
                def _tc(t, sub=sub, t_route=t_route, c=c):
                    wait(t, t_route, L("pst"))
                    return ts.mark(t.transpose(
                        ps_t[0:16, :], comb[:, sub * 16:(sub + 1) * 16],
                        P["ident_f"][:, :]))
                t_tc = em(blk.tensor, _tc)

                def _ec(v, sub=sub, t_tc=t_tc):
                    wait(v, t_tc, L("crow") if sub == 0 else None)
                    return vs.mark(v.tensor_copy(
                        combrow[0:16, sub * 128:(sub + 1) * 128],
                        ps_t[0:16, 0:128]))
                S("pst", em(blk.vector, _ec))

            def _scb(s, c=c):
                wait(s, last["pst"])
                return s2.mark(s.dma_start(
                    out=g["sc_comb"][0:16, c * CH:(c + 1) * CH],
                    in_=combrow[0:16, :]), 16)
            S("crow", em(blk.sync, _scb))

    st3.close()
    b.fence = [vs.tok(), ss.tok(), ts.tok(), xl.tok(), s2.tok(), s3.tok(),
               wl.tok(), gl.tok(), hl.tok()]


def _phase_moe(nc, b, g, P):
    fence = b.fence
    wl = b.sem("mowl")
    xl = b.sem("moxl")
    cl = b.sem("mocl")
    vs = b.sem("mov")
    ss = b.sem("mos")
    ts = b.sem("mot")
    s2 = b.sem("mos2")
    import contextlib
    last = {}

    def L(key):
        return last.get(key)

    def S(key, tok):
        last[key] = tok
        return tok

    with contextlib.ExitStack() as sk:
        SB = lambda n, sh, dt: sk.enter_context(nc.sbuf_tensor(n, list(sh), dt))
        acc = SB("mo_acc", [128, 16 * 4 * CH], BF16)
        wupb = SB("mo_wu", [128, 16 * MOE_I], BF16)
        wdnb = SB("mo_wd", [128, 8 * D], BF16)
        xeb = SB("mo_xe", [128, 16 * CH], BF16)
        ab = [SB(f"mo_ab{i}", [128, 8 * CH], BF16) for i in range(2)]
        rl = [SB(f"mo_rl{i}", [128, CH], F32) for i in range(2)]
        am = SB("mo_am", [128, CH], F32)
        crep = SB("mo_cr", [128, CH], F32)
        rb = [SB(f"mo_rb{i}", [128, CH], F32) for i in range(2)]
        otb = [SB(f"mo_ot{i}", [128, CH], F32) for i in range(2)]
        ps_up = [sk.enter_context(nc.psum_tensor(f"mo_pu{i}", [128, CH], F32))
                 for i in range(2)]
        ps_dn = [sk.enter_context(nc.psum_tensor(f"mo_pd{i}", [128, CH], F32))
                 for i in range(2)]

        with nc.Block() as blk:
            def _ini(v):
                wait(v, *fence)
                i = None
                for q in range(4):
                    i = v.memset(acc[:, q * 16 * CH:(q + 1) * 16 * CH], 0.0)
                return vs.mark(i)
            em(blk.vector, _ini)

            for e in range(20):
                def _lwu(gp, e=e):
                    if e == 0:
                        wait(gp, *fence)
                    wait(gp, L("wupb"))
                    if e < NEXP:
                        src = g["wup_cat"][:, e * MOE_I:(e + 1) * MOE_I]
                    else:
                        src = g["sharedup"][:, (e - NEXP) * MOE_I:
                                            (e - NEXP + 1) * MOE_I]
                    return wl.mark(gp.dma_start(
                        out=wupb[:].rearrange("p (t m) -> p t m", m=MOE_I),
                        in_=src.rearrange("(t p) m -> p t m", p=128)), 16)
                t_wu = em(blk.gpsimd, _lwu)

                def _lwd(gp, e=e):
                    if e == 0:
                        wait(gp, *fence)
                    wait(gp, L("wdnb"))
                    if e < NEXP:
                        src = g["wdown_cat"][e * MOE_I:(e + 1) * MOE_I, :]
                    else:
                        src = g["shareddown"][(e - NEXP) * MOE_I:
                                              (e - NEXP + 1) * MOE_I, :]
                    return wl.mark(gp.dma_start(
                        out=wdnb[:].rearrange("p (t m) -> p t m", m=D),
                        in_=src.rearrange("(t p) m -> p t m", p=128)), 16)
                t_wd = em(blk.gpsimd, _lwd)

                for c in range(4):
                    def _lxe(s, e=e, c=c):
                        if e == 0 and c == 0:
                            wait(s, *fence)
                        wait(s, L("xeb"))
                        return xl.mark(s.dma_start(
                            out=xeb[:].rearrange("p (t x) -> p t x", x=CH),
                            in_=g["sc_x2w"][:, c * CH:(c + 1) * CH]
                                .rearrange("(t p) x -> p t x", p=128)), 16)
                    t_xe = em(blk.sync, _lxe)

                    t_cr = None
                    if e < NEXP:
                        def _lcr(s, e=e, c=c):
                            if e == 0 and c == 0:
                                wait(s, *fence)
                            wait(s, L("crep"))
                            src = bass.AP(g["sc_comb"].tensor,
                                          e * POST0 + c * CH, [[0, 128], [1, CH]])
                            return cl.mark(s.dma_start(out=crep[:, :], in_=src),
                                           16)
                        t_cr = em(blk.sync, _lcr)

                    aw = ab[c % 2]
                    t_umm_last = None
                    t_am_last = None
                    for it in range(8):
                        qq = it % 2

                        def _umm(t, it=it, qq=qq, t_wu=t_wu, t_xe=t_xe):
                            wait(t, t_wu, t_xe, L(("psup", qq)))
                            for kt in range(16):
                                i = t.matmul(
                                    ps_up[qq][:, :],
                                    wupb[:, kt * MOE_I + it * 128:
                                         kt * MOE_I + (it + 1) * 128],
                                    xeb[:, kt * CH:(kt + 1) * CH],
                                    start=(kt == 0), stop=(kt == 15))
                            return ts.mark(i)
                        t_um = em(blk.tensor, _umm)
                        t_umm_last = t_um

                        def _rel(s, qq=qq, t_um=t_um):
                            wait(s, t_um, L(("rl", qq)))
                            return ss.mark(s.activation(rl[qq][:, :],
                                                        ps_up[qq][:, :], AF.Relu,
                                                        bias=0.0, scale=1.0))
                        t_rl = em(blk.scalar, _rel)
                        S(("psup", qq), t_rl)

                        def _am(v, e=e, it=it, qq=qq, c=c, t_rl=t_rl, t_cr=t_cr):
                            wait(v, t_rl)
                            if it == 0:
                                wait(v, L(("ab_free", c % 2)))
                            if e < NEXP:
                                wait(v, t_cr)
                                v.tensor_tensor(am[:, :], rl[qq][:, :],
                                                crep[:, :], ALU.mult)
                                i = v.tensor_tensor(_t(ab[c % 2], it, CH),
                                                    am[:, :], rl[qq][:, :],
                                                    ALU.mult)
                            else:
                                i = v.tensor_tensor(_t(ab[c % 2], it, CH),
                                                    rl[qq][:, :], rl[qq][:, :],
                                                    ALU.mult)
                            return vs.mark(i)
                        t_am = em(blk.vector, _am)
                        S(("rl", qq), t_am)
                        t_am_last = t_am
                    S("xeb", t_umm_last)
                    S("crep", t_am_last)

                    t_dmm_last = None
                    for mt in range(16):
                        qq = mt % 2

                        def _dmm(t, mt=mt, qq=qq, c=c, t_wd=t_wd,
                                 t_am_last=t_am_last):
                            wait(t, t_wd, t_am_last, L(("psdn", qq)))
                            for kt in range(8):
                                i = t.matmul(
                                    ps_dn[qq][:, :],
                                    wdnb[:, kt * D + mt * 128:
                                         kt * D + (mt + 1) * 128],
                                    _t(ab[c % 2], kt, CH),
                                    start=(kt == 0), stop=(kt == 7))
                            return ts.mark(i)
                        t_dm = em(blk.tensor, _dmm)
                        t_dmm_last = t_dm

                        def _acc(v, mt=mt, c=c, qq=qq, t_dm=t_dm):
                            wait(v, t_dm)
                            sl = acc[:, (mt * 4 + c) * CH:(mt * 4 + c + 1) * CH]
                            return vs.mark(v.tensor_tensor(sl, sl,
                                                           ps_dn[qq][:, :],
                                                           ALU.add))
                        S(("psdn", qq), em(blk.vector, _acc))
                    S(("ab_free", c % 2), t_dmm_last)
                S("wupb", t_umm_last)
                S("wdnb", t_dmm_last)

            for c in range(4):
                for mt in range(16):
                    pp = mt % 2

                    def _lr1(s, mt=mt, pp=pp, c=c):
                        wait(s, L(("rb", pp)))
                        return xl.mark(s.dma_start(
                            out=rb[pp][:, :],
                            in_=g["sc_r1"][mt * 128:(mt + 1) * 128,
                                           c * CH:(c + 1) * CH]), 16)
                    t_r = em(blk.sync, _lr1)

                    def _fin(v, mt=mt, pp=pp, c=c, t_r=t_r):
                        wait(v, t_r, L(("otb", pp)))
                        return vs.mark(v.tensor_tensor(
                            otb[pp][:, :], rb[pp][:, :],
                            acc[:, (mt * 4 + c) * CH:(mt * 4 + c + 1) * CH],
                            ALU.add))
                    t_f = em(blk.vector, _fin)
                    S(("rb", pp), t_f)

                    def _sto(s, mt=mt, pp=pp, c=c, t_f=t_f):
                        wait(s, t_f)
                        return s2.mark(s.dma_start(
                            out=g["out_t"][mt * 128:(mt + 1) * 128,
                                           c * CH:(c + 1) * CH],
                            in_=otb[pp][:, :]), 16)
                    S(("otb", pp), em(blk.sync, _sto))

    b.fence = [vs.tok(), ss.tok(), ts.tok(), wl.tok(), xl.tok(), cl.tok(),
               s2.tok()]


def _ptile(a, C=1):
    a = np.asarray(a, np.float32).reshape(-1, C)
    n = a.shape[0] // 128
    return np.ascontiguousarray(np.transpose(a.reshape(n, 128, C), (1, 0, 2))
                                .reshape(128, n * C))


def _prep_shared(inp):
    f32 = np.float32
    c = lambda a: np.ascontiguousarray(np.asarray(a), dtype=f32)
    w = {}
    w["inproj_T"] = c(np.asarray(inp["L0_attn_in_proj_w"]).T)
    w["conv_w"] = _ptile(inp["L0_attn_conv_w"], 4)
    w["conv_b"] = _ptile(inp["L0_attn_conv_b"])
    xp = np.zeros((DI, 256), f32)
    xpT = np.asarray(inp["L0_attn_x_proj_w"], f32).T
    xp[:, 0:128] = xpT[:, 0:128]
    xp[:, 128:144] = xpT[:, 128:144]
    xp[:, 192:208] = xpT[:, 144:160]
    w["xproj_Tp"] = c(xp)
    w["dtproj_T"] = c(np.asarray(inp["L0_attn_dt_proj_w"]).T)
    w["dtproj_b"] = _ptile(inp["L0_attn_dt_proj_b"])
    w["A_log_t"] = _ptile(inp["L0_attn_A_log"], NST)
    w["D_par"] = _ptile(inp["L0_attn_D_param"])
    w["mambaout_T"] = c(np.asarray(inp["L0_attn_mamba_out_w"]).T)
    w["mamba_nw"] = _ptile(inp["L0_attn_mamba_norm_w"])
    w["attn_nw"] = _ptile(inp["L0_attn_attn_norm_w"])
    w["mlp_nw"] = _ptile(inp["L0_attn_mlp_norm_w"])
    w["wqkv_T"] = c(np.asarray(inp["L0_attn_wqkv_w"]).T)
    w["wqkv_b"] = _ptile(inp["L0_attn_wqkv_b"])
    w["outproj_T"] = c(np.asarray(inp["L0_attn_out_proj_w"]).T)
    w["outproj_b"] = _ptile(inp["L0_attn_out_proj_b"])
    sw = np.asarray(inp["L0_attn_subln_w"], f32)
    sp = np.zeros((128, 2), f32)
    sp[:, 0] = sw[0:128]
    sp[:, 1] = sw[128:256]
    w["subln_pat"] = sp
    w["lam_q1"] = c(np.asarray(inp["L0_attn_lambda_q1"]).reshape(128, 1))
    w["lam_k1"] = c(np.asarray(inp["L0_attn_lambda_k1"]).reshape(128, 1))
    w["lam_q2"] = c(np.asarray(inp["L0_attn_lambda_q2"]).reshape(128, 1))
    w["lam_k2"] = c(np.asarray(inp["L0_attn_lambda_k2"]).reshape(128, 1))
    w["fc1_T"] = c(np.asarray(inp["L0_attn_fc1_w"]).T)
    w["fc2_T"] = c(np.asarray(inp["L0_attn_fc2_w"]).T)
    w["router_T"] = c(np.asarray(inp["L0_ffn_router_weight"]).T)
    w["ebias_rep"] = c(np.tile(np.asarray(inp["L0_ffn_e_bias"], f32).reshape(1, NEXP),
                               (128, 1)))
    w["wup_cat"] = c(np.transpose(np.asarray(inp["L0_ffn_w_up"]), (1, 0, 2))
                     .reshape(D, NEXP * MOE_I))
    w["wdown_cat"] = c(np.asarray(inp["L0_ffn_w_down"]).reshape(NEXP * MOE_I, D))
    w["sharedup"] = c(inp["L0_ffn_shared_up"])
    w["shareddown"] = c(inp["L0_ffn_shared_down"])
    w["ident"] = np.eye(128, dtype=bfloat16)
    w["ident_f"] = np.eye(128, dtype=f32)
    r = np.arange(128)
    bnd = np.zeros((128, 256), np.float32)
    bnd[:, 0:128][r[None, :] > r[:, None]] = NEG
    bnd[:, 128:256][r[None, :] <= r[:, None]] = NEG
    w["attn_bnd"] = bnd.astype(bfloat16)
    return w


def _prep_core(inp, bidx, sigma):
    f32 = np.float32
    d = {}
    xt = np.asarray(inp["L0_attn_x"][bidx], f32).T
    xs = np.zeros((D, SEQ), f32)
    if sigma == 0:
        xs[:, :] = xt
    else:
        xs[:, sigma:] = xt[:, :SEQ - sigma]
    d["x_sl"] = np.ascontiguousarray(xs)
    valid = (np.arange(SEQ) >= sigma).astype(f32)
    d["valid128"] = np.ascontiguousarray(np.tile(valid[None, :], (128, 1)))
    npn = np.zeros((128, 16), f32)
    if sigma > 0:
        for qt in range(16):
            base = sigma - qt * 128
            for qr in range(128):
                npn[qr, qt] = -float(max(0, base - (qr + 1)))
    d["npadneg"] = npn
    return d


def make_in_maps(inputs):
    w = _prep_shared(inputs)
    in_maps = []
    for c in range(8):
        bidx, role = c // 2, c % 2
        sigma = WINDOW if role == 0 else 0
        m = dict(w)
        m.update(_prep_core(inputs, bidx, sigma))
        in_maps.append(m)
    return in_maps


def kernel(**inputs):
    in_maps = make_in_maps(inputs)
    nc = build_program()
    res = run_bass_kernel_spmd(nc, in_maps, core_ids=list(range(8)))
    out = np.zeros((BATCH, SEQ, D), np.float32)
    for c in range(8):
        bidx, role = c // 2, c % 2
        t0 = 0 if role == 0 else POST0
        out[bidx, t0:t0 + POST0, :] = res.results[c]["out_t"].T
    return out
